# version 04
# baseline (speedup 1.0000x reference)
import os
import random
import numpy as np
import ml_dtypes
import concourse.bass as bass
import concourse.mybir as mybir
from concourse.bass_utils import run_bass_kernel_spmd

F32 = mybir.dt.float32
BF16 = mybir.dt.bfloat16
I32 = mybir.dt.int32
AF = mybir.ActivationFunctionType
ALU = mybir.AluOpType
AX = mybir.AxisListType
BF = ml_dtypes.bfloat16

D = 2048
H = 16
DH = 128
IF = 8192
SEQ = 8192
HALF = 4096
QUAR = 2048
HT = 32
QT = 16
SCALE_B = DH ** -0.5
LN_EPS = 1e-5
LORA_SC = 0.5

EVEN_BLOCKS = [0, 3, 4, 7, 8, 11, 12, 15]
ODD_BLOCKS = [1, 2, 5, 6, 9, 10, 13, 14]
SLOT_TILE = [4 * b + j for b in EVEN_BLOCKS for j in range(4)] + [
    4 * b + j for b in ODD_BLOCKS for j in range(4)
]
LOC_TILE = {
    0: [4 * b + j for b in EVEN_BLOCKS for j in range(4)],
    1: [4 * b + j for b in ODD_BLOCKS for j in range(4)],
}
MAXG = [max(LOC_TILE[0][i], LOC_TILE[1][i]) for i in range(HT)]
MING = [min(LOC_TILE[0][i], LOC_TILE[1][i]) for i in range(HT)]
ALLOWED_SLOTS = [[s for s in range(64) if SLOT_TILE[s] <= MAXG[i]] for i in range(HT)]
BOUND_SLOTS = [
    [s for s in ALLOWED_SLOTS[i] if SLOT_TILE[s] >= MING[i]] for i in range(HT)
]

MAX_POLLS = 3000000
SETTLE = 100
ENGS = ("tensor", "vector", "scalar", "gpsimd", "sync")


def _host_tables(parity):
    blocks = EVEN_BLOCKS if parity == 0 else ODD_BLOCKS
    pos = np.concatenate([np.arange(512 * b, 512 * b + 512) for b in blocks])
    inv_freq = (1.0 / 10000.0 ** (np.arange(0, DH, 2, dtype=np.float32) / DH)).astype(
        np.float32
    )
    t = pos.astype(np.float32)[:, None] * inv_freq[None, :]
    c = np.cos(t).T.astype(np.float32)
    sn = np.sin(t).T.astype(np.float32)
    rope = np.empty((2, 128, HALF), dtype=np.float32)
    rope[0, 0:64] = c
    rope[0, 64:128] = c
    rope[1, 0:64] = -sn
    rope[1, 64:128] = sn

    masks = np.zeros((HT, 5, 128, 128), dtype=BF)
    tri_kt = np.tril(np.ones((128, 128), dtype=np.float32)).T.astype(BF)
    for i in range(HT):
        qg = LOC_TILE[parity][i]
        for jj, s in enumerate(BOUND_SLOTS[i]):
            kt = SLOT_TILE[s]
            if kt < qg:
                masks[i, jj] = 1.0
            elif kt == qg:
                masks[i, jj] = tri_kt
    return rope, masks


class Carve:
    def __init__(self, tensor, total):
        self.t = tensor
        self.total = total
        self.off = 0
        self.marks = []

    def take(self, cols, parts=128):
        assert self.off + cols <= self.total, (self.off, cols, self.total)
        v = self.t[0:parts, self.off : self.off + cols]
        self.off += cols
        return v

    def push(self):
        self.marks.append(self.off)

    def pop(self):
        self.off = self.marks.pop()


class Builder:

    def __init__(self, debug=()):
        self.debug = set(debug)
        nc = bass.Bass("TRN2", target_bir_lowering=False, debug=False)
        self.nc = nc
        self.dbg_outs = {}
        from contextlib import ExitStack

        self.ctx = ExitStack()
        self.cnt = {}
        self.S = {}
        for sn in ("ld", "wld", "mm", "tp", "act", "dve", "pool", "st", "ps", "bar"):
            self.S[sn] = self.ctx.enter_context(nc.semaphore(sn))
            self.cnt[sn] = 0

        self._declare_io()
        self._declare_scratch()
        self._declare_sbuf()

    def add(self, sem, k):
        self.cnt[sem] += k
        return self.cnt[sem]

    def get(self, sem):
        return self.cnt[sem]

    def snap(self):
        return dict(self.cnt)

    def _declare_io(self):
        nc = self.nc

        def inp(name, shape, dt=F32):
            return nc.dram_tensor(name, shape, dt, kind="ExternalInput")

        self.x_in = inp("x_own", [HALF, D])
        self.w_qkv = inp("w_qkv", [D, 3 * D])
        self.w_out = inp("w_out", [D, D])
        self.ln_scale = inp("ln_scale", [DH])
        self.ln_bias = inp("ln_bias", [DH])
        self.Wb = {n: inp(f"W{n}", [D, D]) for n in "qkvo"}
        self.bb = {n: inp(f"b{n}", [D]) for n in "qkvo"}
        self.Bb = {n: inp(f"B{n}", [16, D]) for n in "qkvo"}
        self.Ab = {n: inp(f"A{n}", [D, 16]) for n in "qkvo"}
        self.w_gate = inp("w_gate", [D, IF])
        self.w_up = inp("w_up", [D, IF])
        self.w_down = inp("w_down", [IF, D])
        self.rope_in = inp("rope", [2, 128, HALF])
        self.masks_in = inp("masks", [HT, 5, 128, 128], BF16)
        self.cfg_in = inp("cfg", [1, 4], I32)
        self.y_out = nc.dram_tensor("y", [QT, 128, D], F32, kind="ExternalOutput")
        self.ident_bf = nc.inline_tensor(np.eye(128, dtype=BF), name="ident_bf")
        self.ident_f32 = nc.inline_tensor(np.eye(128, dtype=np.float32), name="ident_f32")

    def dbg(self, name, shape, dt=F32):
        if name in self.debug:
            t = self.nc.dram_tensor("dbg_" + name, shape, dt, kind="ExternalOutput")
            self.dbg_outs[name] = t
            return t
        return None

    def _declare_scratch(self):
        nc = self.nc

        def scr(name, shape, dt=BF16):
            return nc.dram_tensor(name, shape, dt, kind="Internal")

        self.wqkv_t = scr("wqkv_t", [48, 16, 128, 128])
        self.wqkv_ts = scr("wqkv_ts", [32, 16, 128, 128])
        self.wbt = {n: scr(f"w{n}b_t", [16, 16, 128, 128]) for n in "qkvo"}
        self.wout_bf = scr("wout_bf", [D, D])
        self.wg_perm = scr("wg_perm", [D, IF])
        self.wu_perm = scr("wu_perm", [D, IF])
        self.wd_bf = scr("wd_bf", [IF, D])
        self.wb_bf = {n: scr(f"w{n}_bf_od", [D, D]) for n in "qkvo"}
        self.loraBt = {n: scr(f"loraBt_{n}", [16, 128, 16]) for n in "qkvo"}
        self.loraAt = {n: scr(f"loraAt_{n}", [16, 16, 128]) for n in "qkvo"}
        self.bo_bc_d = scr("bo_bc_d", [128, D], F32)

        self.q_a_loc = scr("q_a_loc", [H, 128, HALF])
        self.aoT_a = scr("aoT_a", [HT, H, 128, 128])
        self.xres2 = scr("xres2", [HT, 128, D], F32)
        self.x2T_d = scr("x2T_d", [16, 128, HALF])
        self.hs_d = scr("hs_d", [64, 128, HALF])
        self.xres3 = scr("xres3", [HT, 128, D], F32)
        self.x3T_d = scr("x3T_d", [16, 128, HALF])
        self.x3Tq_d = scr("x3Tq_d", [16, 128, QUAR])
        self.x3q_d = scr("x3q_d", [QT, 128, D], F32)
        self.q_b_loc = scr("q_b_loc", [H, 128, QUAR])
        self.aoT_b = scr("aoT_b", [QT, H, 128, 128])
        self.x4q = scr("x4q", [QT, 128, D], F32)
        self.x4T_d = scr("x4T_d", [16, 128, QUAR])
        self.hs2_d = scr("hs2_d", [64, 128, QUAR])

        def sh(name, shape, dt=BF16):
            return nc.dram_tensor(name, shape, dt, kind="Internal", addr_space="Shared")

        self.k_a_sh = sh("k_a_sh", [H, 128, SEQ])
        self.v_a_sh = sh("v_a_sh", [H, 128, 64 * 129])
        self.k_b_sh = sh("k_b_sh", [H, 128, SEQ])
        self.v_b_sh = sh("v_b_sh", [H, 128, 64 * 129])
        self.fpub = nc.dram_tensor(
            "fpub", [2, 1], I32, kind="Internal", addr_space="Shared"
        )

    def _declare_sbuf(self):
        nc = self.nc
        self.sb_bf = self.ctx.enter_context(nc.sbuf_tensor("sb_bf", [128, 88576], BF16))
        self.sb_f32 = self.ctx.enter_context(nc.sbuf_tensor("sb_f32", [128, 6912], F32))
        self.sb_misc = self.ctx.enter_context(
            nc.sbuf_tensor("sb_misc", [128, 512], F32)
        )
        self.sb_i32 = self.ctx.enter_context(nc.sbuf_tensor("sb_i32", [1, 16], I32))
        self.cbf = Carve(self.sb_bf, 88576)
        self.cf32 = Carve(self.sb_f32, 6912)
        self.cmisc = Carve(self.sb_misc, 512)

        self.ident_sb = self.cbf.take(128)
        self.identf_sb = self.cf32.take(128)
        self.ones_col = self.cbf.take(1)
        self.ones_row_p = self.cbf.take(128)
        self.eps_sb = self.cmisc.take(1)
        self.lnsc_sb = self.cmisc.take(1)
        self.lnbi_sb = self.cmisc.take(1)
        self.b_qkv_sb = {n: self.cmisc.take(16) for n in "qkv"}

    def barrier(self, blk):
        tgt = 5 * (self.add("bar", 1) // 1)
        S = self.S

        def body(e):
            e.sem_inc(S["bar"], 1)
            e.wait_ge(S["bar"], tgt)

        for en in ENGS:
            getattr(blk, en)(body)

    def bar_block(self):
        with self.nc.Block() as blk:
            self.barrier(blk)

    def uid(self):
        self._uid = getattr(self, "_uid", 0) + 1
        return self._uid

    def ph0a(self):
        S, ctr = self.S, self
        with self.nc.Block() as blk:

            @blk.sync
            def _(e):
                n = 0
                e.dma_start(out=self.ident_sb, in_=self.ident_bf.ap()).then_inc(
                    S["ld"], 16
                )
                e.dma_start(out=self.identf_sb, in_=self.ident_f32.ap()).then_inc(
                    S["ld"], 16
                )
                n += 1

                e.dma_start(
                    out=self.lnsc_sb,
                    in_=self.ln_scale.ap().rearrange("(p o) -> p o", o=1),
                ).then_inc(S["ld"], 16)
                e.dma_start(
                    out=self.lnbi_sb,
                    in_=self.ln_bias.ap().rearrange("(p o) -> p o", o=1),
                ).then_inc(S["ld"], 16)
                n += 3
                for nm in "qkv":
                    for h in range(H):
                        e.dma_start(
                            out=self.b_qkv_sb[nm][:, h : h + 1],
                            in_=self.bb[nm]
                            .ap()[h * 128 : h * 128 + 128]
                            .rearrange("(p o) -> p o", o=1),
                        ).then_inc(S["ld"], 16)
                        n += 1
                ctr.add("ld", 16 * n)

            @blk.vector
            def _(v):
                v.memset(self.ones_col, 1.0)
                v.memset(self.ones_row_p[0:1, :], 1.0)
                v.memset(self.eps_sb, LN_EPS)
                v.sem_inc(S["pool"], 1)
                ctr.add("pool", 1)

            @blk.gpsimd
            def _(g):
                g.dma_start(out=self.sb_i32[0:1, 0:4], in_=self.cfg_in.ap()).then_inc(
                    S["st"], 16
                )
                ctr.add("st", 16)
                g.wait_ge(S["ld"], ctr.get("ld"))
                g.wait_ge(S["st"], ctr.get("st"))
                self.gr = {}
                for rn in ("par", "pidx", "epoch", "pollcnt", "it", "cond",
                           "fv", "c2", "tgt"):
                    self.gr[rn] = g.alloc_register(rn)
                g.reg_load(self.gr["par"], self.sb_i32[0:1, 0:1])
                g.reg_load(self.gr["pidx"], self.sb_i32[0:1, 1:2])
                g.reg_load(self.gr["epoch"], self.sb_i32[0:1, 2:3])
                g.reg_mov(self.gr["pollcnt"], 0)

        self.bar_block()

    def dual_dma(self, g, out_fn, in_fn, sem="st", count=True):
        S = self.S
        with g.If_eq(self.gr["par"], 0):
            g.dma_start(out=out_fn(0), in_=in_fn(0)).then_inc(S[sem], 16)
        with g.Else():
            g.dma_start(out=out_fn(1), in_=in_fn(1)).then_inc(S[sem], 16)
        g.end_ifs()
        if count:
            self.add(sem, 16)

    def dual_dma_q(self, g, out_fn, in_fn, sem="st"):
        S = self.S
        with g.If_eq(self.gr["pidx"], 0):
            g.dma_start(out=out_fn(0), in_=in_fn(0)).then_inc(S[sem], 16)
        with g.Else():
            g.dma_start(out=out_fn(1), in_=in_fn(1)).then_inc(S[sem], 16)
        g.end_ifs()
        self.add(sem, 16)

    def signal(self, g, k):
        r = self.gr
        g.reg_add(r["tgt"], r["epoch"], k)
        g.reg_save(self.sb_i32[0:1, 8:9], r["tgt"])
        self.dual_dma(
            g,
            lambda p: self.fpub.ap()[p : p + 1, :],
            lambda p: self.sb_i32[0:1, 8:9],
        )

    def poll(self, g, k):
        S, r = self.S, self.gr
        g.reg_add(r["tgt"], r["epoch"], k)
        g.reg_mov(r["it"], 0)
        g.reg_mov(r["cond"], 1)
        with g.While(r["cond"]):
            with g.If_eq(r["par"], 0):
                g.dma_start(
                    out=self.sb_i32[0:1, 9:10], in_=self.fpub.ap()[1:2, :]
                ).then_inc(S["ps"], 16)
            with g.Else():
                g.dma_start(
                    out=self.sb_i32[0:1, 9:10], in_=self.fpub.ap()[0:1, :]
                ).then_inc(S["ps"], 16)
            g.end_ifs()
            g.reg_add(r["pollcnt"], r["pollcnt"], 16)
            g.wait_ge(S["ps"], r["pollcnt"])
            g.reg_load(r["fv"], self.sb_i32[0:1, 9:10])
            g.reg_add(r["it"], r["it"], 1)
            g.reg_alu(r["cond"], r["fv"], r["tgt"], ALU.is_lt)
            g.reg_alu(r["c2"], r["it"], MAX_POLLS, ALU.is_lt)
            g.reg_alu(r["cond"], r["cond"], r["c2"], ALU.logical_and)
        for _ in range(SETTLE):
            g.dma_start(
                out=self.sb_i32[0:1, 10:11], in_=self.fpub.ap()[0:1, :]
            ).then_inc(S["ps"], 16)
            g.reg_add(r["pollcnt"], r["pollcnt"], 16)
            g.wait_ge(S["ps"], r["pollcnt"])

    def ph0b(self):
        S, ctr = self.S, self
        jobs = []
        for r in range(16):
            jobs.append(
                (self.w_out.ap()[r * 128 : (r + 1) * 128, :], "plain",
                 self.wout_bf.ap()[r * 128 : (r + 1) * 128, :], 2048, None)
            )
        for r in range(64):
            jobs.append(
                (self.w_down.ap()[r * 128 : (r + 1) * 128, :], "plain",
                 self.wd_bf.ap()[r * 128 : (r + 1) * 128, :], 2048, None)
            )
        for nm in "qkvo":
            for r in range(16):
                jobs.append(
                    (self.Wb[nm].ap()[r * 128 : (r + 1) * 128, :], "plain",
                     self.wb_bf[nm].ap()[r * 128 : (r + 1) * 128, :], 2048, None)
                )
        for r in range(16):
            for c0 in range(0, 3 * D, 2048):
                extra = (
                    self.wqkv_ts.ap()[c0 // 128 : c0 // 128 + 16, r, :, :]
                    if c0 < 4096
                    else None
                )
                jobs.append(
                    (self.w_qkv.ap()[r * 128 : (r + 1) * 128, c0 : c0 + 2048], "tile",
                     self.wqkv_t.ap()[c0 // 128 : c0 // 128 + 16, r, :, :], 2048,
                     extra)
                )
        for src, dst in ((self.w_gate, self.wg_perm), (self.w_up, self.wu_perm)):
            for r in range(16):
                for c0 in range(0, IF, 2048):
                    jobs.append(
                        (src.ap()[r * 128 : (r + 1) * 128, c0 : c0 + 2048], "perm",
                         dst.ap()[r * 128 : (r + 1) * 128, :]
                         .rearrange("p (j n) -> p j n", j=4)[
                             :, :, c0 // 4 : c0 // 4 + 512
                         ],
                         2048, None)
                    )

        NJ = len(jobs)
        st_cum = [0]
        for jb in jobs:
            st_cum.append(st_cum[-1] + (48 if jb[4] is not None else 16))
        self.cbf.push()
        self.cf32.push()
        fbuf = [self.cf32.take(1024) for _ in range(4)]
        bbuf = [self.cbf.take(2048) for _ in range(4)]
        b = self.snap()
        with self.nc.Block() as blk:

            @blk.sync
            def _(e):
                for i in range(NJ):
                    if i >= 2:
                        e.wait_ge(S["dve"], b["dve"] + i - 1)
                    e.dma_start(
                        out=fbuf[(2 * i) % 4][:, :], in_=jobs[i][0][:, 0:1024]
                    ).then_inc(S["ld"], 16)

            @blk.scalar
            def _(e):
                for i in range(NJ):
                    if i >= 2:
                        e.wait_ge(S["dve"], b["dve"] + i - 1)
                    e.dma_start(
                        out=fbuf[(2 * i + 1) % 4][:, :], in_=jobs[i][0][:, 1024:2048]
                    ).then_inc(S["wld"], 16)

            @blk.vector
            def _(e):
                for i, (src, kind, dst, w, extra) in enumerate(jobs):
                    e.wait_ge(S["ld"], b["ld"] + 16 * (i + 1))
                    e.wait_ge(S["wld"], b["wld"] + 16 * (i + 1))
                    if i >= 4:
                        e.wait_ge(S["st"], b["st"] + st_cum[i - 3])
                    cp = getattr(e, "tensor_copy", None) or e.copy
                    for h in range(2):
                        if kind == "perm":
                            dstv = bbuf[i % 4][:, :w].rearrange(
                                "p (j m) -> p m j", j=4
                            )[:, 256 * h : 256 * (h + 1), :]
                        else:
                            dstv = bbuf[i % 4][:, h * 1024 : (h + 1) * 1024]
                        srcv = fbuf[(2 * i + h) % 4][:, :]
                        mi = cp(dstv, srcv)
                        if h == 1:
                            mi.then_inc(S["dve"], 1)

            @blk.gpsimd
            def _(g):
                for i, (src, kind, dst, w, extra) in enumerate(jobs):
                    g.wait_ge(S["dve"], b["dve"] + i + 1)
                    buf = bbuf[i % 4][:, :w]
                    if kind == "plain":
                        g.dma_start(out=dst, in_=buf).then_inc(S["st"], 16)
                    elif kind == "tile":
                        g.dma_start(
                            out=dst.rearrange("t p c -> p t c"),
                            in_=buf.rearrange("p (t c) -> p t c", c=128),
                        ).then_inc(S["st"], 16)
                        if extra is not None:
                            g.dma_start(
                                out=extra.rearrange("t p c -> p t c")[:, :, 64:128],
                                in_=buf.rearrange("p (t c) -> p t c", c=128)[:, :, 0:64],
                            ).then_inc(S["st"], 16)
                            g.dma_start(
                                out=extra.rearrange("t p c -> p t c")[:, :, 0:64],
                                in_=buf.rearrange("p (t c) -> p t c", c=128)[:, :, 64:128],
                            ).then_inc(S["st"], 16)
                    else:
                        g.dma_start(
                            out=dst, in_=buf.rearrange("p (j m) -> p j m", j=4)
                        ).then_inc(S["st"], 16)
                g.wait_ge(S["st"], b["st"] + st_cum[NJ])

        ctr.add("ld", 16 * NJ)
        ctr.add("wld", 16 * NJ)
        ctr.add("dve", NJ)
        ctr.add("st", st_cum[NJ])
        self.cbf.pop()
        self.cf32.pop()
        self.bar_block()

    def ph0c(self, sub=3):
        S, ctr, nc = self.S, self, self.nc
        self.cbf.push()
        self.cf32.push()
        t_in = self.cbf.take(2048)
        t_out = self.cbf.take(2048)
        tj = [(nm, oc) for nm in "qkvo" for oc in range(16)]
        b = self.snap()
        with (
            nc.psum_tensor("pt0a", [128, 128], BF16) as pt0a,
            nc.psum_tensor("pt0b", [128, 128], BF16) as pt0b,
            nc.Block() as blk,
        ):
            pt0 = [pt0a, pt0b]

            @blk.sync
            def _(e):
                for i, (nm, oc) in enumerate(tj):
                    if i >= 1:
                        e.wait_ge(S["tp"], b["tp"] + i * 16)
                    e.dma_start(
                        out=t_in, in_=self.wb_bf[nm].ap()[oc * 128 : (oc + 1) * 128, :]
                    ).then_inc(S["ld"], 16)

            @blk.tensor
            def _(t):
                for i, (nm, oc) in enumerate(tj):
                    t.wait_ge(S["ld"], b["ld"] + 16 * (i + 1))
                    for k in range(16):
                        idx = i * 16 + k
                        if idx >= 2:
                            t.wait_ge(S["dve"], b["dve"] + idx - 1)
                        t.transpose(
                            pt0[k % 2][:, :],
                            t_in[:, k * 128 : (k + 1) * 128],
                            self.ident_sb,
                        ).then_inc(S["tp"], 1)

            @blk.vector
            def _(v):
                for i, (nm, oc) in enumerate(tj):
                    for k in range(16):
                        idx = i * 16 + k
                        v.wait_ge(S["tp"], b["tp"] + idx + 1)
                        if i >= 1:
                            if k == 0:
                                v.wait_ge(S["st"], b["st"] + 16 * i)
                        v.tensor_copy(
                            t_out[:, k * 128 : (k + 1) * 128],
                            pt0[k % 2][:, :],
                        ).then_inc(S["dve"], 1)

            @blk.gpsimd
            def _(g):
                for i, (nm, oc) in enumerate(tj):
                    g.wait_ge(S["dve"], b["dve"] + (i + 1) * 16)
                    g.dma_start(
                        out=self.wbt[nm].ap()[oc, :, :, :].rearrange("t p c -> p t c"),
                        in_=t_out.rearrange("p (t c) -> p t c", c=128),
                    ).then_inc(S["st"], 16)
                g.wait_ge(S["st"], b["st"] + 16 * len(tj))

        ctr.add("ld", 16 * len(tj))
        ctr.add("tp", 16 * len(tj))
        ctr.add("dve", 16 * len(tj))
        ctr.add("st", 16 * len(tj))
        self.bar_block()
        if sub < 2:
            self.cbf.pop()
            self.cf32.pop()
            return

        lf = self.cf32.take(2048)
        bo_bc = self.cf32.take(2048)
        lb = self.cbf.take(2048)
        lo = self.cbf.take(2048)
        ones_bf = self.cbf.take(128)
        b = self.snap()
        with (
            nc.psum_tensor("pt1a", [128, 128], BF16) as pt1a,
            nc.psum_tensor("pt1b", [128, 128], BF16) as pt1b,
            nc.psum_tensor("pt3a", [128, 16], F32) as pt3a,
            nc.psum_tensor("pt3b", [128, 16], F32) as pt3b,
            nc.psum_tensor("pt2", [128, 512], F32) as pt2,
            nc.Block() as blk,
        ):
            ones_row = self.ident_sb
            seq = []
            for nm in "qkvo":
                seq.append(("B", nm))
                seq.append(("A", nm))

            @blk.sync
            def _(e):
                for i, (kind, nm) in enumerate(seq):
                    if i >= 1:
                        e.wait_ge(S["act"], b["act"] + i)
                    if kind == "B":
                        e.dma_start(out=lf[0:16, :], in_=self.Bb[nm].ap()).then_inc(
                            S["ld"], 16
                        )
                    else:
                        e.dma_start(
                            out=lf[0:128, 0:256].rearrange("p (c r) -> p c r", r=16),
                            in_=self.Ab[nm].ap().rearrange("(c p) r -> p c r", p=128),
                        ).then_inc(S["ld"], 16)
                e.wait_ge(S["tp"], b["tp"] + 16 * len(seq))
                e.dma_start(
                    out=lf[0:1, 0:2048],
                    in_=self.bb["o"].ap().rearrange("(o n) -> o n", o=1),
                ).then_inc(S["ld"], 16)

            @blk.scalar
            def _(e):
                for i, (kind, nm) in enumerate(seq):
                    e.wait_ge(S["ld"], b["ld"] + 16 * (i + 1))
                    if i >= 1:
                        e.wait_ge(S["tp"], b["tp"] + 16 * i)
                    if kind == "B":
                        e.copy(lb[0:16, :], lf[0:16, :]).then_inc(S["act"], 1)
                    else:
                        e.copy(lb[0:128, 0:256], lf[0:128, 0:256]).then_inc(
                            S["act"], 1
                        )
                e.wait_ge(S["ld"], b["ld"] + 16 * (len(seq) + 1))
                e.wait_ge(S["tp"], b["tp"] + 16 * len(seq))
                e.copy(lb[0:1, 0:2048], lf[0:1, 0:2048]).then_inc(S["act"], 1)

            @blk.tensor
            def _(t):
                for i, (kind, nm) in enumerate(seq):
                    t.wait_ge(S["act"], b["act"] + i + 1)
                    for k in range(16):
                        idx = i * 16 + k
                        if idx >= 2:
                            t.wait_ge(S["dve"], b["dve"] + idx - 1)
                        if kind == "B":
                            t.matmul(
                                (pt3a if k % 2 == 0 else pt3b)[:, :],
                                lb[0:16, k * 128 : (k + 1) * 128],
                                self.ident_sb[0:16, 0:16],
                                start=True,
                                stop=True,
                            ).then_inc(S["tp"], 1)
                        else:
                            t.transpose(
                                (pt1a if k % 2 == 0 else pt1b)[0:16, 0:128],
                                lb[0:128, k * 16 : (k + 1) * 16],
                                self.ident_sb[:, 0:128],
                            ).then_inc(S["tp"], 1)
                t.wait_ge(S["act"], b["act"] + len(seq) + 1)
                t.wait_ge(S["pool"], b["pool"] + 1)
                for c in range(4):
                    t.matmul(
                        pt2[:, :],
                        ones_bf[0:1, 0:128],
                        lb[0:1, c * 512 : (c + 1) * 512],
                        start=True,
                        stop=True,
                    ).then_inc(S["mm"], 1)
                    t.wait_ge(S["dve"], b["dve"] + 16 * len(seq) + c + 1)

            @blk.vector
            def _(v):
                v.memset(ones_bf[0:1, 0:128], 1.0).then_inc(S["pool"], 1)
                for i, (kind, nm) in enumerate(seq):
                    for k in range(16):
                        idx = i * 16 + k
                        v.wait_ge(S["tp"], b["tp"] + idx + 1)
                        if i >= 1 and k == 0:
                            v.wait_ge(S["st"], b["st"] + 16 * i)
                        if kind == "B":
                            v.tensor_copy(
                                lo[:, k * 16 : (k + 1) * 16],
                                (pt3a if k % 2 == 0 else pt3b)[:, :],
                            ).then_inc(S["dve"], 1)
                        else:
                            v.tensor_copy(
                                lo[0:16, k * 128 : (k + 1) * 128],
                                (pt1a if k % 2 == 0 else pt1b)[0:16, 0:128],
                            ).then_inc(S["dve"], 1)
                for c in range(4):
                    v.wait_ge(S["mm"], b["mm"] + c + 1)
                    v.tensor_copy(
                        bo_bc[:, c * 512 : (c + 1) * 512], pt2[:, :]
                    ).then_inc(S["dve"], 1)

            @blk.gpsimd
            def _(g):
                for i, (kind, nm) in enumerate(seq):
                    g.wait_ge(S["dve"], b["dve"] + 16 * (i + 1))
                    if kind == "B":
                        g.dma_start(
                            out=self.loraBt[nm].ap().rearrange("c p r -> p c r"),
                            in_=lo[:, 0:256].rearrange("p (c r) -> p c r", r=16),
                        ).then_inc(S["st"], 16)
                    else:
                        g.dma_start(
                            out=self.loraAt[nm].ap().rearrange("c p r -> p c r"),
                            in_=lo[0:16, :].rearrange("p (c r) -> p c r", r=128),
                        ).then_inc(S["st"], 16)
                g.wait_ge(S["st"], b["st"] + 16 * len(seq))
                g.wait_ge(S["dve"], b["dve"] + 16 * len(seq) + 4)
                g.dma_start(out=self.bo_bc_d.ap(), in_=bo_bc).then_inc(S["st"], 16)
                g.wait_ge(S["st"], b["st"] + 16 * len(seq) + 16)

        ctr.add("ld", 16 * (len(seq) + 1))
        ctr.add("act", len(seq) + 1)
        ctr.add("pool", 1)
        ctr.add("tp", 16 * len(seq))
        ctr.add("mm", 4)
        ctr.add("dve", 16 * len(seq) + 4)
        ctr.add("st", 16 * len(seq) + 16)
        self.cbf.pop()
        self.cf32.pop()
        self.bar_block()

        t = self.dbg("wbt_q", [16, 16, 128, 128], BF16)
        if t is not None:
            self._dump_dram(self.wbt["q"], t)
        t = self.dbg("wqkv_t", [48, 16, 128, 128], BF16)
        if t is not None:
            self._dump_dram(self.wqkv_t, t)
        t = self.dbg("wg_perm", [D, IF], BF16)
        if t is not None:
            self._dump_dram(self.wg_perm, t)
        t = self.dbg("loraBt_q", [16, 128, 16], BF16)
        if t is not None:
            self._dump_dram(self.loraBt["q"], t)
        t = self.dbg("loraAt_q", [16, 16, 128], BF16)
        if t is not None:
            self._dump_dram(self.loraAt["q"], t)
        t = self.dbg("bo_bc", [128, D], F32)
        if t is not None:
            self._dump_dram(self.bo_bc_d, t)


    def ph1(self):
        S, ctr, nc = self.S, self, self.nc
        self._xT_base = self.cbf.off
        self.x1T = [self.cbf.take(HALF) for _ in range(16)]
        self.cbf.push()
        self.cf32.push()
        xt = [self.cf32.take(2048), self.cf32.take(2048)]
        sq = self.cbf.take(2048)
        xh = [self.cbf.take(2048), self.cbf.take(2048)]
        stm = self.cmisc.take(1)
        sts = self.cmisc.take(1)
        stv = self.cmisc.take(1)
        stq = self.cmisc.take(1)
        str_ = self.cmisc.take(1)
        b = self.snap()
        NDV = 23
        with (
            nc.psum_tensor("p1a", [128, 128], BF16) as p1a,
            nc.psum_tensor("p1b", [128, 128], BF16) as p1b,
            nc.Block() as blk,
        ):
            ptp = [p1a, p1b]

            @blk.sync
            def _(e):
                for it in range(HT):
                    if it >= 2:
                        e.wait_ge(S["dve"], b["dve"] + NDV * (it - 2) + 7)
                        e.wait_ge(S["act"], b["act"] + 2 * (it - 2) + 1)
                    e.dma_start(
                        out=xt[it % 2],
                        in_=self.x_in.ap()[it * 128 : (it + 1) * 128, :],
                    ).then_inc(S["ld"], 16)

            @blk.scalar
            def _(a):
                for it in range(HT):
                    a.wait_ge(S["ld"], b["ld"] + 16 * (it + 1))
                    if it >= 1:
                        a.wait_ge(S["dve"], b["dve"] + NDV * (it - 1) + 4)
                    a.activation(
                        sq, xt[it % 2], AF.Square, accum_out=sts
                    ).then_inc(S["act"], 1)
                    a.wait_ge(S["dve"], b["dve"] + NDV * it + 5)
                    a.activation(stq, stv, AF.Sqrt, bias=self.eps_sb).then_inc(S["act"], 1)

            @blk.vector
            def _(v):
                for it in range(HT):
                    v.wait_ge(S["ld"], b["ld"] + 16 * (it + 1))
                    v.reduce_sum(stm, xt[it % 2], axis=AX.X).then_inc(S["dve"], 1)
                    v.wait_ge(S["dve"], b["dve"] + NDV * it + 1)
                    v.tensor_scalar_mul(stm, stm, 1.0 / D).then_inc(S["dve"], 1)
                    v.wait_ge(S["act"], b["act"] + 2 * it + 1)
                    v.wait_ge(S["dve"], b["dve"] + NDV * it + 2)
                    v.tensor_mul(stv, stm, stm).then_inc(S["dve"], 1)
                    v.wait_ge(S["dve"], b["dve"] + NDV * it + 3)
                    v.scalar_tensor_tensor(
                        stv, sts, 1.0 / D, stv, ALU.mult, ALU.subtract
                    ).then_inc(S["dve"], 1)
                    v.sem_inc(S["dve"], 1)
                    v.wait_ge(S["act"], b["act"] + 2 * it + 2)
                    v.reciprocal(str_, stq).then_inc(S["dve"], 1)
                    v.wait_ge(S["dve"], b["dve"] + NDV * it + 6)
                    if it >= 2:
                        v.wait_ge(S["tp"], b["tp"] + 16 * (it - 1))
                    v.tensor_scalar(
                        xh[it % 2], xt[it % 2], stm, str_, ALU.subtract, ALU.mult
                    ).then_inc(S["dve"], 1)
                    for k in range(16):
                        v.wait_ge(S["tp"], b["tp"] + 16 * it + k + 1)
                        v.tensor_copy(
                            self.x1T[k][:, it * 128 : (it + 1) * 128], ptp[k % 2][:, :]
                        ).then_inc(S["dve"], 1)

            @blk.tensor
            def _(t):
                for it in range(HT):
                    t.wait_ge(S["dve"], b["dve"] + NDV * it + 7)
                    for k in range(16):
                        if 16 * it + k >= 2:
                            t.wait_ge(S["dve"], b["dve"] + NDV * it + 7 + max(0, k - 1))
                        t.transpose(
                            ptp[k % 2][:, :],
                            xh[it % 2][:, k * 128 : (k + 1) * 128],
                            self.ident_sb,
                        ).then_inc(S["tp"], 1)

            @blk.gpsimd
            def _(g):
                g.wait_ge(S["dve"], b["dve"] + NDV * HT)

        ctr.add("ld", 16 * HT)
        ctr.add("act", 2 * HT)
        ctr.add("dve", NDV * HT)
        ctr.add("tp", 16 * HT)
        self.cbf.pop()
        self.cf32.pop()
        self.bar_block()

        t = self.dbg("x1T", [16, 128, HALF], BF16)
        if t is not None:
            for k in range(16):
                self._dump_sbuf_big(self.x1T[k], t.ap()[k])

    def _dump_sbuf_big(self, view, dstap):
        S, ctr = self.S, self
        b = self.snap()
        with self.nc.Block() as blk:

            @blk.gpsimd
            def _(g):
                g.dma_start(out=dstap, in_=view).then_inc(S["st"], 16)
                g.wait_ge(S["st"], b["st"] + 16)

        ctr.add("st", 16)
        self.bar_block()

    def ph2(self):
        S, ctr, nc = self.S, self, self.nc
        self.cbf.push()
        self.cf32.push()
        w_sb = [self.cbf.take(2048), self.cbf.take(2048)]
        wsw_sb = [self.cbf.take(2048), self.cbf.take(2048)]
        part_out = self.cbf.take(HALF)
        vimg = self.cbf.take(32 * 129)
        q2bf = self.cbf.take(512)
        qrbf = self.cbf.take(512)
        rbf = self.cbf.take(1024)
        cs_buf = [self.cf32.take(1024), self.cf32.take(1024)]
        vf = [self.cf32.take(512), self.cf32.take(512)]
        ro_o = self.cf32.take(512)
        n1 = self.cf32.take(512)
        n2 = self.cf32.take(512)
        rowA = self.cf32.take(512)
        rowB = self.cf32.take(512)
        rowC = self.cf32.take(512)

        NH = int(os.environ.get("MK_PH2H", str(H)))
        sub = os.environ.get("MK_PH2SUB", "full")
        parts = {"full": ("q", "k", "v"), "nov": ("q", "k"), "vonly": ("v",)}[sub]
        jobs = [(h, part) for h in range(NH) for part in parts]

        def mmn(part):
            return 24 if part != "v" else 8

        def dvn(part):
            return 96 if part != "v" else 32

        def acn(part):
            return 24 if part != "v" else 8

        def pon(part):
            return 8 if part != "v" else 0

        def tpn(part):
            return 0 if part != "v" else 32

        def stn(part):
            return 128 if part != "v" else 16

        def ldn(part):
            return 160 if part != "v" else 16

        b = self.snap()
        mm_b, dv_b, ac_b, po_b, tp_b, st_b, ld_b = ([b["mm"]], [b["dve"]],
            [b["act"]], [b["pool"]], [b["tp"]], [b["st"]], [b["ld"]])
        for (h, part) in jobs:
            mm_b.append(mm_b[-1] + mmn(part))
            dv_b.append(dv_b[-1] + dvn(part))
            ac_b.append(ac_b[-1] + acn(part))
            po_b.append(po_b[-1] + pon(part))
            tp_b.append(tp_b[-1] + tpn(part))
            st_b.append(st_b[-1] + stn(part))
            ld_b.append(ld_b[-1] + ldn(part))
        prev_qk = []
        last = None
        for j, (h, part) in enumerate(jobs):
            prev_qk.append(last)
            if part != "v":
                last = j

        PN = {"q": 0, "k": 1, "v": 2}
        with (
            nc.psum_tensor("pa0", [128, 512], F32) as pa0,
            nc.psum_tensor("pa1", [128, 512], F32) as pa1,
            nc.psum_tensor("psw0", [128, 512], F32) as psw0,
            nc.psum_tensor("psw1", [128, 512], F32) as psw1,
            nc.psum_tensor("pst1", [128, 512], F32) as pst1,
            nc.psum_tensor("pst2", [128, 512], F32) as pst2,
            nc.psum_tensor("pbc1", [128, 512], F32) as pbc1,
            nc.psum_tensor("pbc2", [128, 512], F32) as pbc2,
            nc.Block() as blk,
        ):
            pacc = [pa0, pa1]
            pswp = [psw0, psw1]
            pbc = [pbc1, pbc2]

            @blk.sync
            def _(e):
                for j, (h, part) in enumerate(jobs):
                    cb = PN[part] * 16 + h
                    if j >= 2:
                        e.wait_ge(S["mm"], mm_b[j - 2] + mmn(jobs[j - 2][1]))
                    e.dma_start(
                        out=w_sb[j % 2].rearrange("p (k c) -> p k c", c=128),
                        in_=self.wqkv_t.ap()[cb].rearrange("k p c -> p k c"),
                    ).then_inc(S["ld"], 16)
                    if part == "v":
                        continue
                    if os.environ.get("MK_NOSWAP"):
                        e.dma_start(
                            out=w_sb[j % 2][0:1, 0:16],
                            in_=self.wqkv_t.ap().flatten()[0:16].rearrange(
                                "(o c) -> o c", o=1
                            ),
                        ).then_inc(S["ld"], 16)
                    else:
                        e.dma_start(
                            out=wsw_sb[j % 2].rearrange("p (k c) -> p k c", c=128),
                            in_=self.wqkv_ts.ap()[cb].rearrange("k p c -> p k c"),
                        ).then_inc(S["ld"], 16)
                    for tc in range(8):
                        if tc >= 2:
                            e.wait_ge(S["dve"], dv_b[j] + 12 * (tc - 2) + 2)
                        elif prev_qk[j] is not None:
                            e.wait_ge(
                                S["dve"], dv_b[prev_qk[j]] + 12 * (6 + tc) + 2
                            )
                        if os.environ.get("MK_NOCS"):
                            e.dma_start(
                                out=cs_buf[tc % 2][0:1, 0:16],
                                in_=self.rope_in.ap().flatten()[0:16].rearrange(
                                    "(o c) -> o c", o=1
                                ),
                            ).then_inc(S["ld"], 16)
                        else:
                            e.dma_start(
                                out=cs_buf[tc % 2].rearrange("p (u c) -> p u c", u=2),
                                in_=self.rope_in.ap()[
                                    :, :, tc * 512 : (tc + 1) * 512
                                ].rearrange("u p c -> p u c"),
                            ).then_inc(S["ld"], 16)

            @blk.tensor
            def _(t):
                for j, (h, part) in enumerate(jobs):
                    t.wait_ge(S["ld"], ld_b[j] + (32 if part != "v" else 16))
                    if part == "v":
                        t.wait_ge(S["dve"], dv_b[j])
                        for tc in range(8):
                            if tc >= 2:
                                t.wait_ge(S["act"], ac_b[j] + tc - 1)
                            elif prev_qk[j] is not None and j >= 1:
                                t.wait_ge(
                                    S["dve"],
                                    dv_b[prev_qk[j]] + 12 * (6 + tc) + 2,
                                )
                            for k in range(16):
                                mi = t.matmul(
                                    pacc[tc % 2][:, :],
                                    w_sb[j % 2][:, k * 128 : (k + 1) * 128],
                                    self.x1T[k][:, tc * 512 : (tc + 1) * 512],
                                    start=(k == 0),
                                    stop=(k == 15),
                                )
                                if k == 15:
                                    mi.then_inc(S["mm"], 1)
                            t.wait_ge(S["act"], ac_b[j] + tc + 1)
                            for u in range(4):
                                idx = 4 * tc + u
                                if idx >= 2:
                                    t.wait_ge(S["dve"], dv_b[j] + idx - 1)
                                t.transpose(
                                    pbc[idx % 2][:, 0:128],
                                    vf[tc % 2][:, u * 128 : (u + 1) * 128],
                                    self.identf_sb,
                                ).then_inc(S["tp"], 1)
                        continue
                    for tc in range(8):
                        if tc >= 2:
                            t.wait_ge(S["dve"], dv_b[j] + 12 * (tc - 2) + 2)
                        elif j >= 1:
                            pj = j - 1
                            if jobs[pj][1] == "v":
                                t.wait_ge(S["act"], ac_b[pj] + (6 + tc) + 1)
                            else:
                                t.wait_ge(S["dve"], dv_b[pj] + 12 * (6 + tc) + 2)
                        noswap = bool(os.environ.get("MK_NOSWAP"))
                        for k in range(16):
                            mi = t.matmul(
                                pacc[tc % 2][:, :],
                                w_sb[j % 2][:, k * 128 : (k + 1) * 128],
                                self.x1T[k][:, tc * 512 : (tc + 1) * 512],
                                start=(k == 0),
                                stop=(k == 15),
                            )
                            if k == 15 and noswap:
                                mi.then_inc(S["mm"], 1)
                        if not noswap:
                            for k in range(16):
                                mi = t.matmul(
                                    pswp[tc % 2][:, :],
                                    wsw_sb[j % 2][:, k * 128 : (k + 1) * 128],
                                    self.x1T[k][:, tc * 512 : (tc + 1) * 512],
                                    start=(k == 0),
                                    stop=(k == 15),
                                )
                                if k == 15:
                                    mi.then_inc(S["mm"], 1)
                        if os.environ.get("MK_NOSTATS"):
                            t.sem_inc(S["mm"], 1)
                            t.sem_inc(S["mm"], 1)
                            continue
                        t.wait_ge(S["act"], ac_b[j] + 3 * tc + 2)
                        if tc >= 1:
                            t.wait_ge(S["dve"], dv_b[j] + 12 * (tc - 1) + 5)
                        t.matmul(
                            pst1[0:1, :], self.ones_col, qrbf, start=True, stop=True
                        )
                        t.matmul(
                            pst2[0:1, :], self.ones_col, q2bf, start=True, stop=True
                        ).then_inc(S["mm"], 1)
                        t.wait_ge(S["dve"], dv_b[j] + 12 * tc + 9)
                        t.matmul(
                            pbc1[:, :],
                            self.ones_row_p[0:1, :],
                            rbf[0:1, 0:512],
                            start=True,
                            stop=True,
                        )
                        t.matmul(
                            pbc2[:, :],
                            self.ones_row_p[0:1, :],
                            rbf[0:1, 512:1024],
                            start=True,
                            stop=True,
                        ).then_inc(S["mm"], 1)

            @blk.vector
            def _(v):
                for j, (h, part) in enumerate(jobs):
                    if part == "v":
                        v.wait_ge(S["st"], st_b[j])
                        v.memset(vimg, 1.0)
                        for tc in range(8):
                            for u in range(4):
                                idx = 4 * tc + u
                                v.wait_ge(S["tp"], tp_b[j] + idx + 1)
                                v.tensor_copy(
                                    vimg[:, idx * 129 : idx * 129 + 128],
                                    pbc[idx % 2][:, 0:128],
                                ).then_inc(S["dve"], 1)
                        continue
                    for tc in range(8):
                        v.wait_ge(S["mm"], mm_b[j] + 3 * tc + 1)
                        v.wait_ge(S["ld"], ld_b[j] + 32 + 16 * (tc + 1))
                        if tc >= 1:
                            v.wait_ge(S["pool"], po_b[j] + tc)
                        if os.environ.get("MK_NOROPE"):
                            v.sem_inc(S["dve"], 1)
                            v.sem_inc(S["dve"], 1)
                        else:
                            v.tensor_mul(
                                n1, pacc[tc % 2][:, :], cs_buf[tc % 2][:, 0:512]
                            ).then_inc(S["dve"], 1)
                            v.tensor_mul(
                                n2, pswp[tc % 2][:, :], cs_buf[tc % 2][:, 512:1024]
                            ).then_inc(S["dve"], 1)
                        nostats = bool(os.environ.get("MK_NOSTATS"))
                        v.wait_ge(S["mm"], mm_b[j] + 3 * tc + 2)
                        if nostats:
                            for _ in range(7):
                                v.sem_inc(S["dve"], 1)
                        else:
                            v.tensor_scalar_mul(rowA[0:1, :], pst1[0:1, :], 1.0 / DH).then_inc(
                                S["dve"], 1
                            )
                            v.wait_ge(S["dve"], dv_b[j] + 12 * tc + 3)
                            v.tensor_mul(rowB[0:1, :], rowA[0:1, :], rowA[0:1, :]).then_inc(
                                S["dve"], 1
                            )
                            v.wait_ge(S["dve"], dv_b[j] + 12 * tc + 4)
                            v.scalar_tensor_tensor(
                                rowB[0:1, :],
                                pst2[0:1, :],
                                1.0 / DH,
                                rowB[0:1, :],
                                ALU.mult,
                                ALU.subtract,
                            ).then_inc(S["dve"], 1)
                            v.wait_ge(S["act"], ac_b[j] + 3 * tc + 3)
                            v.reciprocal(rowB[0:1, :], rowC[0:1, :]).then_inc(S["dve"], 1)
                            v.wait_ge(S["dve"], dv_b[j] + 12 * tc + 6)
                            v.tensor_mul(rowC[0:1, :], rowA[0:1, :], rowB[0:1, :]).then_inc(
                                S["dve"], 1
                            )
                            v.wait_ge(S["dve"], dv_b[j] + 12 * tc + 7)
                            v.tensor_copy(rbf[0:1, 0:512], rowB[0:1, :]).then_inc(
                                S["dve"], 1
                            )
                            v.tensor_copy(rbf[0:1, 512:1024], rowC[0:1, :]).then_inc(
                                S["dve"], 1
                            )
                        v.wait_ge(S["mm"], mm_b[j] + 3 * tc + 3)
                        v.wait_ge(S["pool"], po_b[j] + tc + 1)
                        if j >= 1 and tc == 0:
                            v.wait_ge(S["st"], st_b[j])
                        if nostats:
                            v.sem_inc(S["dve"], 1)
                            v.sem_inc(S["dve"], 1)
                            src_no = (
                                pacc[tc % 2][:, :]
                                if os.environ.get("MK_NOROPE")
                                else ro_o
                            )
                            v.tensor_copy(
                                part_out[:, tc * 512 : (tc + 1) * 512], src_no
                            ).then_inc(S["dve"], 1)
                        else:
                            v.tensor_mul(n1, ro_o, pbc1[:, :]).then_inc(S["dve"], 1)
                            v.tensor_sub(n2, n1, pbc2[:, :]).then_inc(S["dve"], 1)
                            v.tensor_scalar(
                                part_out[:, tc * 512 : (tc + 1) * 512],
                                n2,
                                self.lnsc_sb,
                                self.lnbi_sb,
                                ALU.mult,
                                ALU.add,
                            ).then_inc(S["dve"], 1)

            @blk.scalar
            def _(a):
                for j, (h, part) in enumerate(jobs):
                    if part == "v":
                        if j >= 3:
                            a.wait_ge(S["tp"], tp_b[j])
                        for tc in range(8):
                            a.wait_ge(S["mm"], mm_b[j] + tc + 1)
                            if tc >= 2:
                                a.wait_ge(S["tp"], tp_b[j] + 4 * (tc - 1))
                            a.activation(
                                vf[tc % 2],
                                pacc[tc % 2][:, :],
                                AF.Copy,
                            ).then_inc(S["act"], 1)
                        continue
                    for tc in range(8):
                        a.wait_ge(S["pool"], po_b[j] + tc + 1)
                        if tc >= 1:
                            a.wait_ge(S["mm"], mm_b[j] + 3 * (tc - 1) + 2)
                        else:
                            a.wait_ge(S["mm"], mm_b[j])
                        if os.environ.get("MK_NOSTATS"):
                            for _ in range(3):
                                a.sem_inc(S["act"], 1)
                            continue
                        a.activation(q2bf, ro_o, AF.Square).then_inc(S["act"], 1)
                        a.activation(qrbf, ro_o, AF.Copy).then_inc(S["act"], 1)
                        a.wait_ge(S["dve"], dv_b[j] + 12 * tc + 5)
                        a.activation(
                            rowC[0:1, :], rowB[0:1, :], AF.Sqrt, bias=self.eps_sb[0:1, :]
                        ).then_inc(S["act"], 1)

            @blk.gpsimd
            def _(g):
                for j, (h, part) in enumerate(jobs):
                    if part == "v":
                        g.wait_ge(S["dve"], dv_b[j] + 32)
                        self.dual_dma(
                            g,
                            lambda p, h=h: self.v_a_sh.ap()[h][
                                :, p * 4128 : (p + 1) * 4128
                            ],
                            lambda p: vimg,
                            count=False,
                        )
                        continue
                    for tc in range(8):
                        g.wait_ge(S["dve"], dv_b[j] + 12 * tc + 2)
                        if tc >= 1:
                            g.wait_ge(S["act"], ac_b[j] + 3 * (tc - 1) + 2)
                        if os.environ.get("MK_NOROPE"):
                            g.sem_inc(S["pool"], 1)
                        else:
                            g.tensor_add(ro_o, n1, n2).then_inc(S["pool"], 1)
                        g.wait_ge(S["dve"], dv_b[j] + 12 * tc + 12)
                        sl = part_out[:, tc * 512 : (tc + 1) * 512]
                        if part == "q" or os.environ.get("MK_NODUAL"):
                            g.dma_start(
                                out=self.q_a_loc.ap()[h][:, tc * 512 : (tc + 1) * 512],
                                in_=sl,
                            ).then_inc(S["st"], 16)
                        else:
                            self.dual_dma(
                                g,
                                lambda p, h=h, tc=tc: self.k_a_sh.ap()[h][
                                    :,
                                    p * 4096 + tc * 512 : p * 4096 + (tc + 1) * 512,
                                ],
                                lambda p, sl=sl: sl,
                                count=False,
                            )
                g.wait_ge(S["st"], st_b[len(jobs)])
                self.signal(g, 1)

        for nm, tbl in (("mm", mm_b), ("dve", dv_b), ("act", ac_b),
                        ("pool", po_b), ("tp", tp_b), ("ld", ld_b)):
            ctr.cnt[nm] = tbl[len(jobs)]
        ctr.cnt["st"] = st_b[len(jobs)] + 16
        self.cbf.pop()
        self.cf32.pop()
        self.bar_block()

        t = self.dbg("k_sh", [H, 128, SEQ], BF16)
        if t is not None:
            self._dump_dram(self.k_a_sh, t)
        t = self.dbg("v_sh", [H, 128, 64 * 129], BF16)
        if t is not None:
            self._dump_dram(self.v_a_sh, t)
        t = self.dbg("q_loc", [H, 128, HALF], BF16)
        if t is not None:
            self._dump_dram(self.q_a_loc, t)

    def ph3(self, k):
        S, ctr = self.S, self
        with self.nc.Block() as blk:

            @blk.gpsimd
            def _(g):
                self.poll(g, k)

        self.bar_block()

    def ph4(self):
        S, ctr, nc = self.S, self, self.nc
        self.cbf.off = self._xT_base
        self.cbf.push()
        self.cf32.push()
        K_sb = [self.cbf.take(SEQ), self.cbf.take(SEQ)]
        V_sb = [self.cbf.take(64 * 129), self.cbf.take(64 * 129)]
        q_sb = [self.cbf.take(HALF), self.cbf.take(HALF)]
        msk = [self.cbf.take(5 * 128), self.cbf.take(5 * 128)]
        a_bf = [self.cbf.take(1024), self.cbf.take(1024)]
        ao_bf = [self.cbf.take(128), self.cbf.take(128)]
        aoT_sb = [self.cbf.take(128), self.cbf.take(128)]
        s2f = [self.cf32.take(1024), self.cf32.take(1024)]
        rdv = self.cmisc.take(1)

        NH4 = int(os.environ.get("MK_PH4H", str(H)))
        CH = 8
        sched = []
        for h in range(NH4):
            for i in range(HT):
                slots = ALLOWED_SLOTS[i]
                chunks = [slots[c : c + CH] for c in range(0, len(slots), CH)]
                sched.append((h, i, chunks))

        b = self.snap()
        marks = []
        cnt = dict(mm=b["mm"], act=b["act"], dve=b["dve"], tp=b["tp"],
                   st=b["st"], ld=b["ld"])
        headld = {}
        for h in range(NH4):
            cnt["ld"] += 48
            headld[h] = cnt["ld"]
        cnt["ld"] = b["ld"]
        prev_h = -1
        for (h, i, chunks) in sched:
            e = {}
            if h != prev_h:
                cnt["ld"] += 48
                prev_h = h
            cnt["ld"] += 16
            e["ld_mask"] = cnt["ld"]
            e["ld_head"] = b["ld"] + 48 * (h + 1)
            e["mm_sc"] = []
            e["mm_av"] = []
            e["act_sq"] = []
            e["dve_sq"] = []
            for c, sl in enumerate(chunks):
                cnt["mm"] += 1
                e["mm_sc"].append(cnt["mm"])
                nb = (len(sl) + 3) // 4
                cnt["act"] += nb
                e["act_sq"].append(cnt["act"])
                nmask = sum(1 for s_ in sl if s_ in BOUND_SLOTS[i])
                cnt["dve"] += 1 + 1 + nmask
                e["dve_sq"].append(cnt["dve"])
                cnt["mm"] += 1
                e["mm_av"].append(cnt["mm"])
            cnt["dve"] += 2
            e["dve_norm"] = cnt["dve"]
            cnt["tp"] += 1
            e["tp"] = cnt["tp"]
            cnt["dve"] += 1
            e["dve_evac"] = cnt["dve"]
            cnt["st"] += 16
            e["st"] = cnt["st"]
            marks.append(e)
        tot = dict(cnt)

        with (
            nc.psum_tensor("c4a", [128, 512], F32) as c4a,
            nc.psum_tensor("c4b", [128, 512], F32) as c4b,
            nc.psum_tensor("c4c", [128, 512], F32) as c4c,
            nc.psum_tensor("c4d", [128, 512], F32) as c4d,
            nc.psum_tensor("pav", [128, 512], F32) as pav,
            nc.psum_tensor("ptr0", [128, 128], BF16) as ptr0,
            nc.psum_tensor("ptr1", [128, 128], BF16) as ptr1,
            nc.Block() as blk,
        ):
            sb = [c4a, c4b, c4c, c4d]
            ptr = [ptr0, ptr1]

            @blk.sync
            def _(e):
                prev = -1
                for n, (h, i, chunks) in enumerate(sched):
                    if h != prev:
                        prev = h
                        if h >= 2:
                            lastm = [m for m in marks[: n] if sched[marks.index(m)][0] == h - 2] if False else None
                            idx = (h - 2 + 1) * HT - 1
                            e.wait_ge(S["mm"], marks[idx]["mm_av"][-1])
                        e.dma_start(
                            out=K_sb[h % 2], in_=self.k_a_sh.ap()[h]
                        ).then_inc(S["ld"], 16)
                        e.dma_start(
                            out=V_sb[h % 2], in_=self.v_a_sh.ap()[h]
                        ).then_inc(S["ld"], 16)
                        e.dma_start(
                            out=q_sb[h % 2], in_=self.q_a_loc.ap()[h]
                        ).then_inc(S["ld"], 16)
                    if n >= 2:
                        e.wait_ge(S["dve"], marks[n - 2]["dve_sq"][-1])
                    e.dma_start(
                        out=msk[n % 2].rearrange("p (j c) -> p j c", c=128),
                        in_=self.masks_in.ap()[i].rearrange("j p c -> p j c"),
                    ).then_inc(S["ld"], 16)

            @blk.tensor
            def _(t):
                for n, (h, i, chunks) in enumerate(sched):
                    e = marks[n]
                    t.wait_ge(S["ld"], e["ld_head"])
                    nsl = sum(len(c) for c in chunks)
                    done = 0
                    for c, sl in enumerate(chunks):
                        if c >= 2:
                            t.wait_ge(S["act"], e["act_sq"][c - 2])
                        elif n >= 1:
                            pe = marks[n - 1]
                            if len(pe["act_sq"]) >= 2 - c:
                                t.wait_ge(S["act"], pe["act_sq"][len(pe["act_sq"]) - (2 - c)])
                        for p_, s_ in enumerate(sl):
                            mi = t.matmul(
                                sb[2 * (c % 2) + p_ // 4][:, (p_ % 4) * 128 : (p_ % 4) * 128 + 128],
                                K_sb[h % 2][:, s_ * 128 : (s_ + 1) * 128],
                                q_sb[h % 2][:, i * 128 : (i + 1) * 128],
                                start=True,
                                stop=True,
                            )
                            if p_ == len(sl) - 1:
                                mi.then_inc(S["mm"], 1)
                        t.wait_ge(S["dve"], e["dve_sq"][c])
                        if c == 0 and n >= 1:
                            t.wait_ge(S["dve"], marks[n - 1]["dve_norm"])
                        for p_, s_ in enumerate(sl):
                            mi = t.matmul(
                                pav[:, 0:129],
                                a_bf[c % 2][:, p_ * 128 : (p_ + 1) * 128],
                                V_sb[h % 2][:, s_ * 129 : s_ * 129 + 129],
                                start=(done == 0),
                                stop=(done == nsl - 1),
                            )
                            done += 1
                            if p_ == len(sl) - 1:
                                mi.then_inc(S["mm"], 1)
                    t.wait_ge(S["dve"], e["dve_norm"])
                    if n >= 2:
                        t.wait_ge(S["dve"], marks[n - 2]["dve_evac"])
                    t.transpose(
                        ptr[n % 2][:, :], ao_bf[n % 2], self.ident_sb
                    ).then_inc(S["tp"], 1)

            @blk.scalar
            def _(a):
                for n, (h, i, chunks) in enumerate(sched):
                    e = marks[n]
                    for c, sl in enumerate(chunks):
                        a.wait_ge(S["mm"], e["mm_sc"][c])
                        if c >= 2:
                            a.wait_ge(S["dve"], e["dve_sq"][c - 2])
                        elif n >= 1:
                            pe = marks[n - 1]
                            if len(pe["dve_sq"]) >= 2 - c:
                                a.wait_ge(S["dve"], pe["dve_sq"][len(pe["dve_sq"]) - (2 - c)])
                        nb = (len(sl) + 3) // 4
                        for bk in range(nb):
                            w = min(512, (len(sl) - bk * 4) * 128)
                            a.activation(
                                s2f[c % 2][:, bk * 512 : bk * 512 + w],
                                sb[2 * (c % 2) + bk][:, 0:w],
                                AF.Square,
                            ).then_inc(S["act"], 1)

            @blk.vector
            def _(v):
                for n, (h, i, chunks) in enumerate(sched):
                    e = marks[n]
                    base_dve = (marks[n - 1]["dve_evac"] if n >= 1 else b["dve"])
                    dv = base_dve
                    for c, sl in enumerate(chunks):
                        v.wait_ge(S["act"], e["act_sq"][c])
                        if c >= 2:
                            v.wait_ge(S["mm"], e["mm_av"][c - 2])
                        elif n >= 1:
                            pe = marks[n - 1]
                            if len(pe["mm_av"]) >= 2 - c:
                                v.wait_ge(S["mm"], pe["mm_av"][len(pe["mm_av"]) - (2 - c)])
                        v.tensor_mul(
                            a_bf[c % 2][:, 0 : len(sl) * 128],
                            s2f[c % 2][:, 0 : len(sl) * 128],
                            s2f[c % 2][:, 0 : len(sl) * 128],
                        ).then_inc(S["dve"], 1)
                        dv += 1
                        v.wait_ge(S["dve"], dv)
                        v.sem_inc(S["dve"], 1)
                        dv += 1
                        if n >= 2:
                            v.wait_ge(S["ld"], e["ld_mask"])
                        else:
                            v.wait_ge(S["ld"], e["ld_mask"])
                        for p_, s_ in enumerate(sl):
                            if s_ in BOUND_SLOTS[i]:
                                jj = BOUND_SLOTS[i].index(s_)
                                v.tensor_mul(
                                    a_bf[c % 2][:, p_ * 128 : (p_ + 1) * 128],
                                    a_bf[c % 2][:, p_ * 128 : (p_ + 1) * 128],
                                    msk[n % 2][:, jj * 128 : (jj + 1) * 128],
                                ).then_inc(S["dve"], 1)
                                dv += 1
                    v.wait_ge(S["mm"], e["mm_av"][-1])
                    v.reciprocal(rdv, pav[:, 128:129]).then_inc(S["dve"], 1)
                    dv += 1
                    v.wait_ge(S["dve"], dv)
                    if n >= 2:
                        v.wait_ge(S["tp"], marks[n - 2]["tp"])
                    v.tensor_scalar_mul(
                        ao_bf[n % 2], pav[:, 0:128], rdv
                    ).then_inc(S["dve"], 1)
                    dv += 1
                    v.wait_ge(S["tp"], e["tp"])
                    if n >= 2:
                        v.wait_ge(S["st"], marks[n - 2]["st"])
                    v.tensor_copy(aoT_sb[n % 2], ptr[n % 2][:, :]).then_inc(
                        S["dve"], 1
                    )

            @blk.gpsimd
            def _(g):
                for n, (h, i, chunks) in enumerate(sched):
                    e = marks[n]
                    g.wait_ge(S["dve"], e["dve_evac"])
                    g.dma_start(
                        out=self.aoT_a.ap()[i, h, :, :], in_=aoT_sb[n % 2]
                    ).then_inc(S["st"], 16)

        for nm in ("mm", "act", "dve", "tp", "st", "ld"):
            ctr.cnt[nm] = tot[nm]
        self.cbf.pop()
        self.cf32.pop()
        self.bar_block()

        t = self.dbg("aoT", [HT, H, 128, 128], BF16)
        if t is not None:
            self._dump_dram(self.aoT_a, t)

    def ph5(self):
        S, ctr, nc = self.S, self, self.nc
        self.cbf.push()
        self.cf32.push()
        w_sb = self.cbf.take(16 * 2048)
        aot = [self.cbf.take(2048), self.cbf.take(2048)]
        xh2 = [self.cbf.take(2048), self.cbf.take(2048)]
        gat = [self.cbf.take(2048), self.cbf.take(2048)]
        sq = self.cbf.take(2048)
        xt = [self.cf32.take(2048), self.cf32.take(2048)]
        x2t = self.cf32.take(2048)
        stm = self.cmisc.take(1)
        sts = self.cmisc.take(1)
        stv = self.cmisc.take(1)
        stq = self.cmisc.take(1)
        str_ = self.cmisc.take(1)

        b = self.snap()
        NMM = 64 + 16
        NDV = 28
        with (
            nc.psum_tensor("pj0", [128, 512], F32) as pj0,
            nc.psum_tensor("pj1", [128, 512], F32) as pj1,
            nc.psum_tensor("pt5a", [128, 128], BF16) as pt5a,
            nc.psum_tensor("pt5b", [128, 128], BF16) as pt5b,
            nc.Block() as blk,
        ):
            pj = [pj0, pj1]
            ptr = [pt5a, pt5b]

            @blk.sync
            def _(e):
                for k in range(16):
                    e.dma_start(
                        out=w_sb[:, k * 2048 : (k + 1) * 2048],
                        in_=self.wout_bf.ap()[k * 128 : (k + 1) * 128, :],
                    ).then_inc(S["ld"], 16)
                for i in range(HT):
                    if i >= 2:
                        e.wait_ge(S["mm"], b["mm"] + 20 * (i - 1))
                        e.wait_ge(S["dve"], b["dve"] + NDV * (i - 2) + 4)
                    e.dma_start(
                        out=aot[i % 2].rearrange("p (h c) -> p h c", c=128),
                        in_=self.aoT_a.ap()[i].rearrange("h p c -> p h c"),
                    ).then_inc(S["ld"], 16)
                    e.dma_start(
                        out=xt[i % 2], in_=self.x_in.ap()[i * 128 : (i + 1) * 128, :]
                    ).then_inc(S["ld"], 16)

            @blk.tensor
            def _(t):
                for i in range(HT):
                    t.wait_ge(S["ld"], b["ld"] + 256 + 32 * (i + 1) - 16)
                    for jc in range(4):
                        if jc >= 2:
                            t.wait_ge(S["dve"], b["dve"] + NDV * i + jc - 1)
                        elif i >= 1:
                            t.wait_ge(S["dve"], b["dve"] + NDV * (i - 1) + 2 + jc + 1)
                        for h in range(H):
                            mi = t.matmul(
                                pj[jc % 2][:, :],
                                aot[i % 2][:, h * 128 : (h + 1) * 128],
                                w_sb[:, h * 2048 + jc * 512 : h * 2048 + (jc + 1) * 512],
                                start=(h == 0),
                                stop=(h == 15),
                            )
                            if h == 15:
                                mi.then_inc(S["mm"], 1)
                    t.wait_ge(S["dve"], b["dve"] + NDV * i + 12)
                    for k in range(16):
                        idx = 16 * i + k
                        if idx >= 2:
                            t.wait_ge(S["dve"], b["dve"] + NDV * i + 12 + max(0, k - 1))
                        if k < 2 and i >= 1:
                            t.wait_ge(S["dve"], b["dve"] + NDV * (i - 1) + 13 + 14 + k)
                        t.transpose(
                            ptr[k % 2][:, :],
                            xh2[i % 2][:, k * 128 : (k + 1) * 128],
                            self.ident_sb,
                        ).then_inc(S["tp"], 1)
                    t.nop().then_inc(S["mm"], 16)

            @blk.vector
            def _(v):
                for i in range(HT):
                    dv = b["dve"] + NDV * i
                    v.wait_ge(S["ld"], b["ld"] + 256 + 32 * (i + 1))
                    for jc in range(4):
                        v.wait_ge(S["mm"], b["mm"] + 20 * i + jc + 1)
                        if i >= 1 and jc == 0:
                            v.wait_ge(S["act"], b["act"] + 2 * (i - 1) + 1)
                            v.wait_ge(S["st"], b["st"] + 32 * (i - 1) + 16)
                        v.tensor_add(
                            x2t[:, jc * 512 : (jc + 1) * 512],
                            pj[jc % 2][:, :],
                            xt[i % 2][:, jc * 512 : (jc + 1) * 512],
                        ).then_inc(S["dve"], 1)
                    v.reduce_sum(stm, x2t, axis=AX.X).then_inc(S["dve"], 1)
                    v.wait_ge(S["dve"], dv + 5)
                    v.tensor_scalar_mul(stm, stm, 1.0 / D).then_inc(S["dve"], 1)
                    v.wait_ge(S["act"], b["act"] + 2 * i + 1)
                    v.wait_ge(S["dve"], dv + 6)
                    v.tensor_mul(stv, stm, stm).then_inc(S["dve"], 1)
                    v.wait_ge(S["dve"], dv + 7)
                    v.scalar_tensor_tensor(
                        stv, sts, 1.0 / D, stv, ALU.mult, ALU.subtract
                    ).then_inc(S["dve"], 1)
                    v.wait_ge(S["act"], b["act"] + 2 * i + 2)
                    v.reciprocal(str_, stq).then_inc(S["dve"], 1)
                    v.wait_ge(S["dve"], dv + 9)
                    v.sem_inc(S["dve"], 1)
                    v.sem_inc(S["dve"], 1)
                    if i >= 2:
                        v.wait_ge(S["tp"], b["tp"] + 16 * (i - 1))
                    v.tensor_scalar(
                        xh2[i % 2], x2t, stm, str_, ALU.subtract, ALU.mult
                    ).then_inc(S["dve"], 1)
                    for k in range(16):
                        v.wait_ge(S["tp"], b["tp"] + 16 * i + k + 1)
                        if k == 0 and i >= 2:
                            v.wait_ge(S["st"], b["st"] + 32 * (i - 1))
                        v.tensor_copy(
                            gat[i % 2][:, k * 128 : (k + 1) * 128], ptr[k % 2][:, :]
                        ).then_inc(S["dve"], 1)

            @blk.scalar
            def _(a):
                for i in range(HT):
                    a.wait_ge(S["dve"], b["dve"] + NDV * i + 4)
                    if i >= 1:
                        a.wait_ge(S["dve"], b["dve"] + NDV * (i - 1) + 7)
                    a.activation(sq, x2t, AF.Square, accum_out=sts).then_inc(
                        S["act"], 1
                    )
                    a.wait_ge(S["dve"], b["dve"] + NDV * i + 8)
                    a.activation(stq, stv, AF.Sqrt, bias=self.eps_sb).then_inc(
                        S["act"], 1
                    )

            @blk.gpsimd
            def _(g):
                for i in range(HT):
                    g.wait_ge(S["dve"], b["dve"] + NDV * i + 13)
                    g.dma_start(
                        out=self.xres2.ap()[i], in_=x2t
                    ).then_inc(S["st"], 16)
                    g.wait_ge(S["dve"], b["dve"] + NDV * (i + 1))
                    g.dma_start(
                        out=self.x2T_d.ap()[:, :, i * 128 : (i + 1) * 128].rearrange(
                            "k p c -> p k c"
                        ),
                        in_=gat[i % 2].rearrange("p (k c) -> p k c", c=128),
                    ).then_inc(S["st"], 16)

        ctr.add("ld", 256 + 32 * HT)
        ctr.add("mm", 20 * HT)
        ctr.add("tp", 16 * HT)
        ctr.add("dve", NDV * HT)
        ctr.add("act", 2 * HT)
        ctr.add("st", 32 * HT)
        self.bar_block()
        self.cbf.pop()
        self.cf32.pop()

        t = self.dbg("x2", [HT, 128, D], F32)
        if t is not None:
            self._dump_dram(self.xres2, t)
        t = self.dbg("x2T", [16, 128, HALF], BF16)
        if t is not None:
            self._dump_dram(self.x2T_d, t)

    def ph6a(self, xT_dram, hs_dram, toks):
        S, ctr, nc = self.S, self, self.nc
        nt = toks // 512
        self.cbf.off = self._xT_base
        self.cbf.push()
        self.cf32.push()
        xT = [self.cbf.take(toks) for _ in range(16)]
        wg_sb = self.cbf.take(8192)
        wu_sb = self.cbf.take(8192)
        msk_bf = self.cbf.take(4 * 512)
        hs_sb = self.cbf.take(4 * 512)
        mbt = self.cbf.take(512)
        m2b = self.cbf.take(512)
        hq = self.cf32.take(4 * 512)
        aq = self.cf32.take(4 * 512)
        sg = self.cf32.take(512)
        eqb = self.cf32.take(512)
        dm = self.cf32.take(512)
        m1 = self.cf32.take(512)
        m2 = self.cf32.take(512)

        u6 = self.uid()
        b = self.snap()
        NQ, NT = 16, nt
        with (
            nc.psum_tensor(f"pf0_{u6}", [128, 512], F32) as pf0,
            nc.psum_tensor(f"pf1_{u6}", [128, 512], F32) as pf1,
            nc.psum_tensor(f"pf2_{u6}", [128, 512], F32) as pf2,
            nc.psum_tensor(f"pf3_{u6}", [128, 512], F32) as pf3,
            nc.psum_tensor(f"pf4_{u6}", [128, 512], F32) as pf4,
            nc.psum_tensor(f"pf5_{u6}", [128, 512], F32) as pf5,
            nc.psum_tensor(f"pf6_{u6}", [128, 512], F32) as pf6,
            nc.psum_tensor(f"pf7_{u6}", [128, 512], F32) as pf7,
            nc.Block() as blk,
        ):
            pb = [pf0, pf1, pf2, pf3, pf4, pf5, pf6, pf7]

            def stq(Q, T):
                return Q * NT + T

            @blk.sync
            def _(e):
                for k in range(16):
                    e.dma_start(
                        out=xT[k], in_=xT_dram.ap()[k][:, 0:toks]
                    ).then_inc(S["ld"], 16)
                for Q in range(NQ):
                    if Q >= 1:
                        e.wait_ge(S["mm"], b["mm"] + 8 * NT * Q)
                    for wsb, wdr in ((wg_sb, self.wg_perm), (wu_sb, self.wu_perm)):
                        for k in range(16):
                            e.dma_start(
                                out=wsb[:, k * 512 : (k + 1) * 512].rearrange(
                                    "p (pl c) -> p pl c", pl=4
                                ),
                                in_=wdr.ap()[k * 128 : (k + 1) * 128, :]
                                .rearrange("p (pl n) -> p pl n", pl=4)[
                                    :, :, Q * 128 : (Q + 1) * 128
                                ],
                            ).then_inc(S["ld"], 16)

            @blk.tensor
            def _(t):
                for Q in range(NQ):
                    t.wait_ge(S["ld"], b["ld"] + 256 + 512 * (Q + 1))
                    for T in range(NT):
                        n = stq(Q, T)
                        for pl in range(4):
                            for gu in range(2):
                                bidx = pl * 2 + gu
                                if n >= 1:
                                    t.wait_ge(S["dve"], b["dve"] + 32 * (n - 1) + 4)
                                w = wg_sb if gu == 0 else wu_sb
                                for k in range(16):
                                    mi = t.matmul(
                                        pb[bidx][:, :],
                                        w[:, k * 512 + pl * 128 : k * 512 + (pl + 1) * 128],
                                        xT[k][:, T * 512 : (T + 1) * 512],
                                        start=(k == 0),
                                        stop=(k == 15),
                                    )
                                    if k == 15:
                                        mi.then_inc(S["mm"], 1)

            @blk.scalar
            def _(a):
                for Q in range(NQ):
                    for T in range(NT):
                        n = stq(Q, T)
                        for pl in range(4):
                            a.wait_ge(S["mm"], b["mm"] + 8 * n + 2 * pl + 1)
                            if n >= 1:
                                a.wait_ge(S["dve"], b["dve"] + 32 * (n - 1) + pl + 1)
                            if pl >= 1:
                                a.wait_ge(S["dve"], b["dve"] + 32 * n + pl)
                            a.activation(
                                sg, pb[pl * 2][:, :], AF.Silu
                            ).then_inc(S["act"], 1)
                            a.wait_ge(S["dve"], b["dve"] + 32 * n + pl + 1)
                            a.activation(
                                aq[:, pl * 512 : (pl + 1) * 512],
                                hq[:, pl * 512 : (pl + 1) * 512],
                                AF.Abs,
                            ).then_inc(S["act"], 1)

            @blk.vector
            def _(v):
                for Q in range(NQ):
                    for T in range(NT):
                        n = stq(Q, T)
                        dv = b["dve"] + 32 * n
                        for pl in range(4):
                            v.wait_ge(S["act"], b["act"] + 8 * n + 2 * pl + 1)
                            v.wait_ge(S["mm"], b["mm"] + 8 * n + 2 * pl + 2)
                            v.tensor_mul(
                                hq[:, pl * 512 : (pl + 1) * 512],
                                sg,
                                pb[pl * 2 + 1][:, :],
                            ).then_inc(S["dve"], 1)
                        v.wait_ge(S["act"], b["act"] + 8 * n + 8)
                        v.tensor_tensor(
                            m1, aq[:, 0:512], aq[:, 512:1024], ALU.min
                        ).then_inc(S["pool"], 1)
                        v.tensor_tensor(
                            m2, aq[:, 1024:1536], aq[:, 1536:2048], ALU.min
                        ).then_inc(S["pool"], 1)
                        v.tensor_tensor(m1, m1, m2, ALU.min).then_inc(S["pool"], 1)
                        for pl in range(4):
                            sl2 = slice(pl * 512, (pl + 1) * 512)
                            v.tensor_tensor(
                                eqb, aq[:, sl2], m1, ALU.is_equal
                            ).then_inc(S["dve"], 1)
                            v.wait_ge(S["dve"], dv + 4 + 4 * pl + 1)
                            v.scalar_tensor_tensor(
                                msk_bf[:, sl2],
                                eqb,
                                1.0e30,
                                aq[:, sl2],
                                ALU.mult,
                                ALU.add,
                            ).then_inc(S["dve"], 1)
                            v.tensor_mul(dm, hq[:, sl2], eqb).then_inc(S["dve"], 1)
                            v.wait_ge(S["dve"], dv + 4 + 4 * pl + 3)
                            v.tensor_sub(hq[:, sl2], hq[:, sl2], dm).then_inc(
                                S["dve"], 1
                            )
                        v.tensor_tensor(
                            mbt, msk_bf[:, 0:512], msk_bf[:, 512:1024], ALU.min
                        ).then_inc(S["pool"], 1)
                        v.tensor_tensor(
                            m2b, msk_bf[:, 1024:1536], msk_bf[:, 1536:2048], ALU.min
                        ).then_inc(S["pool"], 1)
                        v.tensor_tensor(m2b, m2b, mbt, ALU.min).then_inc(S["pool"], 1)
                        if n >= 1:
                            v.wait_ge(S["st"], b["st"] + 64 * (n - 1) + 64)
                        for pl in range(4):
                            sl2 = slice(pl * 512, (pl + 1) * 512)
                            v.tensor_tensor(
                                eqb, msk_bf[:, sl2], m2b, ALU.is_equal
                            ).then_inc(S["dve"], 1)
                            v.wait_ge(S["dve"], dv + 20 + 3 * pl + 1)
                            v.tensor_mul(dm, hq[:, sl2], eqb).then_inc(S["dve"], 1)
                            v.wait_ge(S["dve"], dv + 20 + 3 * pl + 2)
                            v.tensor_sub(
                                hs_sb[:, sl2], hq[:, sl2], dm
                            ).then_inc(S["dve"], 1)

            @blk.gpsimd
            def _(g):
                for Q in range(NQ):
                    for T in range(NT):
                        n = stq(Q, T)
                        dv = b["dve"] + 32 * n
                        g.wait_ge(S["dve"], dv + 32)
                        for pl in range(4):
                            g.dma_start(
                                out=hs_dram.ap()[pl * 16 + Q][:, T * 512 : (T + 1) * 512],
                                in_=hs_sb[:, pl * 512 : (pl + 1) * 512],
                            ).then_inc(S["st"], 16)

        ctr.add("ld", 256 + 512 * NQ)
        ctr.add("mm", 8 * NQ * NT)
        ctr.add("act", 8 * NQ * NT)
        ctr.add("dve", 32 * NQ * NT)
        ctr.add("pool", 6 * NQ * NT)
        ctr.add("st", 64 * NQ * NT)
        self.cbf.pop()
        self.cf32.pop()
        self.bar_block()

    def ph6b(self, hs_dram, xprev_dram, xout_dram, toks):
        S, ctr, nc = self.S, self, self.nc
        ntile = toks // 128
        nttg = ntile // 4
        self.cbf.off = self._xT_base
        self.cbf.push()
        self.cf32.push()
        wd_sb = self.cbf.take(64 * 1024)
        hst = [self.cbf.take(512), self.cbf.take(512)]
        xpt = [self.cf32.take(1024) for _ in range(4)]
        y3 = [self.cf32.take(512), self.cf32.take(512)]

        u6 = self.uid()
        b = self.snap()
        with (
            nc.psum_tensor(f"pd0_{u6}", [128, 512], F32) as pd0,
            nc.psum_tensor(f"pd1_{u6}", [128, 512], F32) as pd1,
            nc.psum_tensor(f"pd2_{u6}", [128, 512], F32) as pd2,
            nc.psum_tensor(f"pd3_{u6}", [128, 512], F32) as pd3,
            nc.psum_tensor(f"pd4_{u6}", [128, 512], F32) as pd4,
            nc.psum_tensor(f"pd5_{u6}", [128, 512], F32) as pd5,
            nc.psum_tensor(f"pd6_{u6}", [128, 512], F32) as pd6,
            nc.psum_tensor(f"pd7_{u6}", [128, 512], F32) as pd7,
            nc.Block() as blk,
        ):
            pbk = [pd0, pd1, pd2, pd3, pd4, pd5, pd6, pd7]

            @blk.sync
            def _(e):
                for jh in range(2):
                    if jh == 1:
                        e.wait_ge(S["mm"], b["mm"] + 64 * nttg)
                    for gch in range(64):
                        p_, qch = gch // 16, gch % 16
                        e.dma_start(
                            out=wd_sb[:, gch * 1024 : (gch + 1) * 1024],
                            in_=bass.AP(
                                self.wd_bf,
                                (4 * qch * 128 + p_) * D + jh * 1024,
                                [[4 * D, 128], [1, 1024]],
                            ),
                        ).then_inc(S["ld"], 16)
                    for ttg in range(nttg):
                        sn = jh * nttg + ttg
                        for i in range(64):
                            gi = sn * 64 + i
                            if gi >= 2:
                                e.wait_ge(S["mm"], b["mm"] + gi - 1)
                            e.dma_start(
                                out=hst[i % 2],
                                in_=hs_dram.ap()[i][:, ttg * 512 : (ttg + 1) * 512],
                            ).then_inc(S["ld"], 16)
                        for tt in range(4):
                            if sn >= 1:
                                e.wait_ge(S["dve"], b["dve"] + 8 * sn)
                            e.dma_start(
                                out=xpt[tt],
                                in_=xprev_dram.ap()[ttg * 4 + tt][
                                    :, jh * 1024 : (jh + 1) * 1024
                                ],
                            ).then_inc(S["ld"], 16)

            @blk.tensor
            def _(t):
                for jh in range(2):
                    ldw = b["ld"] + jh * (64 * 16 + nttg * 68 * 16) + 64 * 16
                    t.wait_ge(S["ld"], ldw)
                    for ttg in range(nttg):
                        sn = jh * nttg + ttg
                        for i in range(64):
                            t.wait_ge(S["ld"], ldw + ttg * 68 * 16 + (i + 1) * 16)
                            if i == 0 and sn >= 1:
                                t.wait_ge(S["dve"], b["dve"] + 8 * sn)
                            for tt in range(4):
                                for jq in range(2):
                                    mi = t.matmul(
                                        pbk[tt * 2 + jq][:, :],
                                        hst[i % 2][:, tt * 128 : (tt + 1) * 128],
                                        wd_sb[:, i * 1024 + jq * 512 : i * 1024 + (jq + 1) * 512],
                                        start=(i == 0),
                                        stop=(i == 63),
                                    )
                                    if jq == 1 and tt == 3:
                                        mi.then_inc(S["mm"], 1)

            @blk.vector
            def _(v):
                for jh in range(2):
                    ldw = b["ld"] + jh * (64 * 16 + nttg * 68 * 16) + 64 * 16
                    for ttg in range(nttg):
                        sn = jh * nttg + ttg
                        v.wait_ge(S["mm"], b["mm"] + 64 * (sn + 1))
                        v.wait_ge(S["ld"], ldw + ttg * 68 * 16 + 68 * 16)
                        for tt in range(4):
                            for jq in range(2):
                                idx = tt * 2 + jq
                                if idx >= 2:
                                    v.wait_ge(
                                        S["st"], b["st"] + 128 * sn + 16 * (idx - 1)
                                    )
                                elif sn >= 1:
                                    v.wait_ge(S["st"], b["st"] + 128 * (sn - 1) + 16 * (idx + 7))
                                v.tensor_add(
                                    y3[idx % 2],
                                    pbk[idx][:, :],
                                    xpt[tt][:, jq * 512 : (jq + 1) * 512],
                                ).then_inc(S["dve"], 1)

            @blk.gpsimd
            def _(g):
                for jh in range(2):
                    for ttg in range(nttg):
                        sn = jh * nttg + ttg
                        for tt in range(4):
                            for jq in range(2):
                                idx = tt * 2 + jq
                                g.wait_ge(S["dve"], b["dve"] + 8 * sn + idx + 1)
                                g.dma_start(
                                    out=xout_dram.ap()[ttg * 4 + tt][
                                        :, jh * 1024 + jq * 512 : jh * 1024 + (jq + 1) * 512
                                    ],
                                    in_=y3[idx % 2],
                                ).then_inc(S["st"], 16)

        ctr.add("ld", 2 * (64 * 16 + nttg * 68 * 16))
        ctr.add("mm", 64 * 2 * nttg)
        ctr.add("dve", 8 * 2 * nttg)
        ctr.add("st", 128 * 2 * nttg)
        self.cbf.pop()
        self.cf32.pop()
        self.bar_block()

    def ph7(self, x_dram, xT_dram, ntile):
        S, ctr, nc = self.S, self, self.nc
        self.cbf.off = self._xT_base
        self.cbf.push()
        self.cf32.push()
        xh7 = [self.cbf.take(2048), self.cbf.take(2048)]
        gat = [self.cbf.take(2048), self.cbf.take(2048)]
        sq = self.cbf.take(2048)
        xt = [self.cf32.take(2048), self.cf32.take(2048)]
        stm = self.cmisc.take(1)
        sts = self.cmisc.take(1)
        stv = self.cmisc.take(1)
        stq = self.cmisc.take(1)
        str_ = self.cmisc.take(1)
        u6 = self.uid()
        b = self.snap()
        NDV = 26
        NDV = 25
        with (
            nc.psum_tensor(f"pt7a_{u6}", [128, 128], BF16) as p7a,
            nc.psum_tensor(f"pt7b_{u6}", [128, 128], BF16) as p7b,
            nc.Block() as blk,
        ):
            ptr = [p7a, p7b]

            @blk.sync
            def _(e):
                for i in range(ntile):
                    if i >= 2:
                        e.wait_ge(S["dve"], b["dve"] + NDV * (i - 2) + 9)
                        e.wait_ge(S["act"], b["act"] + 2 * (i - 2) + 1)
                    e.dma_start(out=xt[i % 2], in_=x_dram.ap()[i]).then_inc(
                        S["ld"], 16
                    )

            @blk.scalar
            def _(a):
                for i in range(ntile):
                    a.wait_ge(S["ld"], b["ld"] + 16 * (i + 1))
                    if i >= 1:
                        a.wait_ge(S["dve"], b["dve"] + NDV * (i - 1) + 4)
                    a.activation(sq, xt[i % 2], AF.Square, accum_out=sts).then_inc(
                        S["act"], 1
                    )
                    a.wait_ge(S["dve"], b["dve"] + NDV * i + 4)
                    a.activation(stq, stv, AF.Sqrt, bias=self.eps_sb).then_inc(
                        S["act"], 1
                    )

            @blk.vector
            def _(v):
                for i in range(ntile):
                    dv = b["dve"] + NDV * i
                    v.wait_ge(S["ld"], b["ld"] + 16 * (i + 1))
                    v.reduce_sum(stm, xt[i % 2], axis=AX.X).then_inc(S["dve"], 1)
                    v.wait_ge(S["dve"], dv + 1)
                    v.tensor_scalar_mul(stm, stm, 1.0 / D).then_inc(S["dve"], 1)
                    v.wait_ge(S["act"], b["act"] + 2 * i + 1)
                    v.wait_ge(S["dve"], dv + 2)
                    v.tensor_mul(stv, stm, stm).then_inc(S["dve"], 1)
                    v.wait_ge(S["dve"], dv + 3)
                    v.scalar_tensor_tensor(
                        stv, sts, 1.0 / D, stv, ALU.mult, ALU.subtract
                    ).then_inc(S["dve"], 1)
                    v.sem_inc(S["dve"], 1)
                    v.wait_ge(S["act"], b["act"] + 2 * i + 2)
                    v.reciprocal(str_, stq).then_inc(S["dve"], 1)
                    v.wait_ge(S["dve"], dv + 6)
                    v.sem_inc(S["dve"], 1)
                    v.sem_inc(S["dve"], 1)
                    if i >= 2:
                        v.wait_ge(S["tp"], b["tp"] + 16 * (i - 1))
                    v.tensor_scalar(
                        xh7[i % 2], xt[i % 2], stm, str_, ALU.subtract, ALU.mult
                    ).then_inc(S["dve"], 1)
                    for k in range(16):
                        v.wait_ge(S["tp"], b["tp"] + 16 * i + k + 1)
                        if k == 0 and i >= 2:
                            v.wait_ge(S["st"], b["st"] + 16 * (i - 1))
                        v.tensor_copy(
                            gat[i % 2][:, k * 128 : (k + 1) * 128], ptr[k % 2][:, :]
                        ).then_inc(S["dve"], 1)

            @blk.tensor
            def _(t):
                for i in range(ntile):
                    t.wait_ge(S["dve"], b["dve"] + NDV * i + 9)
                    for k in range(16):
                        idx = 16 * i + k
                        if idx >= 2:
                            t.wait_ge(S["dve"], b["dve"] + NDV * i + 9 + max(0, k - 1))
                        if k < 2 and i >= 1:
                            t.wait_ge(S["dve"], b["dve"] + NDV * (i - 1) + 10 + 14 + k)
                        t.transpose(
                            ptr[k % 2][:, :],
                            xh7[i % 2][:, k * 128 : (k + 1) * 128],
                            self.ident_sb,
                        ).then_inc(S["tp"], 1)

            @blk.gpsimd
            def _(g):
                for i in range(ntile):
                    g.wait_ge(S["dve"], b["dve"] + NDV * (i + 1))
                    g.dma_start(
                        out=xT_dram.ap()[:, :, i * 128 : (i + 1) * 128].rearrange(
                            "k p c -> p k c"
                        ),
                        in_=gat[i % 2].rearrange("p (k c) -> p k c", c=128),
                    ).then_inc(S["st"], 16)

        ctr.add("ld", 16 * ntile)
        ctr.add("act", 2 * ntile)
        ctr.add("dve", NDV * ntile)
        ctr.add("tp", 16 * ntile)
        ctr.add("st", 16 * ntile)
        self.cbf.pop()
        self.cf32.pop()
        self.bar_block()

    def ph7q(self):
        S, ctr = self.S, self
        b = self.snap()
        with self.nc.Block() as blk:

            @blk.gpsimd
            def _(g):
                for k in range(16):
                    self.dual_dma_q(
                        g,
                        lambda p, k=k: self.x3Tq_d.ap()[k],
                        lambda p, k=k: self.x3T_d.ap()[k][
                            :, p * QUAR : (p + 1) * QUAR
                        ],
                    )
                for t in range(QT):
                    self.dual_dma_q(
                        g,
                        lambda p, t=t: self.x3q_d.ap()[t],
                        lambda p, t=t: self.xres3.ap()[p * QT + t],
                    )
                g.wait_ge(S["st"], ctr.get("st"))

        self.bar_block()

    def ph8(self, xT_dram, toks, parts):
        S, ctr, nc = self.S, self, self.nc
        nt = toks // 512
        self.cbf.off = self._xT_base
        self.cbf.push()
        self.cf32.push()
        xT = [self.cbf.take(toks) for _ in range(16)]
        w_sb = [self.cbf.take(2048), self.cbf.take(2048)]
        lb_sb = [self.cbf.take(256), self.cbf.take(256)]
        la_sb = [self.cbf.take(128), self.cbf.take(128)]
        tmp_bf = self.cbf.take(512)
        part_out = self.cbf.take(toks)
        vimg = self.cbf.take((toks // 128) * 129)
        vf = [self.cf32.take(512), self.cf32.take(512)]

        jobs = [(h, part) for h in range(H) for part in parts]
        NM = {"q": 0, "k": 1, "v": 2, "o": 3}

        def mmn(p):
            return 2 * nt

        def acn(p):
            return 2 * nt

        def dvn(p):
            return 4 * nt if p == "v" else 0

        def tpn(p):
            return 4 * nt if p == "v" else 0

        def stn(p):
            return 16 if p == "v" else 16 * nt

        def ldn(p):
            return 16 * 3

        b = self.snap()
        mm_b, ac_b, dv_b, tp_b, st_b, ld_b = ([b["mm"]], [b["act"]], [b["dve"]],
                                              [b["tp"]], [b["st"]], [b["ld"]])
        for (h, p) in jobs:
            mm_b.append(mm_b[-1] + mmn(p))
            ac_b.append(ac_b[-1] + acn(p))
            dv_b.append(dv_b[-1] + dvn(p))
            tp_b.append(tp_b[-1] + tpn(p))
            st_b.append(st_b[-1] + stn(p))
            ld_b.append(ld_b[-1] + ldn(p))

        sfx = f"{toks}_{parts[0]}"
        with (
            nc.psum_tensor(f"p8a{sfx}", [128, 512], F32) as p8a,
            nc.psum_tensor(f"p8b{sfx}", [128, 512], F32) as p8b,
            nc.psum_tensor(f"p8t1{sfx}", [128, 512], F32) as p8t1,
            nc.psum_tensor(f"p8t2{sfx}", [128, 512], F32) as p8t2,
            nc.psum_tensor(f"p8v1{sfx}", [128, 128], F32) as p8v1,
            nc.psum_tensor(f"p8v2{sfx}", [128, 128], F32) as p8v2,
            nc.Block() as blk,
        ):
            pacc = [p8a, p8b]
            ptmp = [p8t1, p8t2]
            pvt = [p8v1, p8v2]

            @blk.sync
            def _(e):
                for k in range(16):
                    e.dma_start(out=xT[k], in_=xT_dram.ap()[k][:, 0:toks]).then_inc(
                        S["ld"], 16
                    )
                for j, (h, p) in enumerate(jobs):
                    if j >= 2:
                        e.wait_ge(S["mm"], mm_b[j - 1])
                    e.dma_start(
                        out=w_sb[j % 2].rearrange("p (k c) -> p k c", c=128),
                        in_=self.wbt[p].ap()[h].rearrange("k p c -> p k c"),
                    ).then_inc(S["ld"], 16)
                    e.dma_start(
                        out=lb_sb[j % 2].rearrange("p (k r) -> p k r", r=16),
                        in_=self.loraBt[p].ap().rearrange("k p r -> p k r"),
                    ).then_inc(S["ld"], 16)
                    e.dma_start(
                        out=la_sb[j % 2][0:16, :], in_=self.loraAt[p].ap()[h]
                    ).then_inc(S["ld"], 16)

            @blk.tensor
            def _(t):
                for j, (h, p) in enumerate(jobs):
                    t.wait_ge(S["ld"], ld_b[j + 1] + 256)
                    for tc in range(nt):
                        if tc >= 2 or j >= 1:
                            t.wait_ge(S["act"], ac_b[j] + 2 * (tc - 2) + 1 if tc >= 2
                                      else ac_b[j] - 2 + 1)
                        for k in range(16):
                            mi = t.matmul(
                                ptmp[tc % 2][0:16, :],
                                lb_sb[j % 2][:, k * 16 : (k + 1) * 16],
                                xT[k][:, tc * 512 : (tc + 1) * 512],
                                start=(k == 0),
                                stop=(k == 15),
                            )
                            if k == 15:
                                mi.then_inc(S["mm"], 1)
                        if tc >= 2:
                            if p == "v":
                                t.wait_ge(S["act"], ac_b[j] + 2 * (tc - 2) + 2)
                            else:
                                t.wait_ge(S["act"], ac_b[j] + 2 * (tc - 2) + 2)
                        elif j >= 1:
                            t.wait_ge(S["act"], ac_b[j] - 2 + 2)
                        for k in range(16):
                            t.matmul(
                                pacc[tc % 2][:, :],
                                w_sb[j % 2][:, k * 128 : (k + 1) * 128],
                                xT[k][:, tc * 512 : (tc + 1) * 512],
                                start=(k == 0),
                                stop=False,
                            )
                        t.wait_ge(S["act"], ac_b[j] + 2 * tc + 1)
                        t.matmul(
                            pacc[tc % 2][:, :],
                            la_sb[j % 2][0:16, :],
                            tmp_bf[0:16, :],
                            start=False,
                            stop=True,
                        ).then_inc(S["mm"], 1)
                        if p == "v":
                            t.wait_ge(S["act"], ac_b[j] + 2 * tc + 2)
                            for u in range(4):
                                idx = 4 * tc + u
                                if idx >= 2:
                                    t.wait_ge(S["dve"], dv_b[j] + idx - 1)
                                elif j >= 3:
                                    t.wait_ge(S["dve"], dv_b[j])
                                t.transpose(
                                    pvt[idx % 2][:, :],
                                    vf[tc % 2][:, u * 128 : (u + 1) * 128],
                                    self.identf_sb,
                                ).then_inc(S["tp"], 1)

            @blk.scalar
            def _(a):
                for j, (h, p) in enumerate(jobs):
                    for tc in range(nt):
                        a.wait_ge(S["mm"], mm_b[j] + 2 * tc + 1)
                        a.activation(
                            tmp_bf[0:16, :], ptmp[tc % 2][0:16, :], AF.Copy,
                            scale=LORA_SC,
                        ).then_inc(S["act"], 1)
                        a.wait_ge(S["mm"], mm_b[j] + 2 * tc + 2)
                        bias = self.b_qkv_sb[p][:, h : h + 1] if p in "qkv" else None
                        af = AF.Identity if bias is not None else AF.Copy
                        if p == "v":
                            if tc >= 2:
                                a.wait_ge(S["tp"], tp_b[j] + 4 * (tc - 1))
                            a.activation(
                                vf[tc % 2], pacc[tc % 2][:, :], af, bias=bias
                            ).then_inc(S["act"], 1)
                        else:
                            if tc == 0 and j >= 1:
                                a.wait_ge(S["st"], st_b[j])
                            a.activation(
                                part_out[:, tc * 512 : (tc + 1) * 512],
                                pacc[tc % 2][:, :],
                                af,
                                bias=bias,
                            ).then_inc(S["act"], 1)

            @blk.vector
            def _(v):
                for j, (h, p) in enumerate(jobs):
                    if p != "v":
                        continue
                    v.wait_ge(S["st"], st_b[j])
                    v.memset(vimg, 1.0)
                    for tc in range(nt):
                        for u in range(4):
                            idx = 4 * tc + u
                            v.wait_ge(S["tp"], tp_b[j] + idx + 1)
                            v.tensor_copy(
                                vimg[:, idx * 129 : idx * 129 + 128],
                                pvt[idx % 2][:, :],
                            ).then_inc(S["dve"], 1)

            @blk.gpsimd
            def _(g):
                for j, (h, p) in enumerate(jobs):
                    if p == "v":
                        g.wait_ge(S["dve"], dv_b[j + 1])
                        self.dual_dma(
                            g,
                            lambda pp, h=h: self.v_b_sh.ap()[h][
                                :, pp * (toks // 128) * 129 : (pp + 1) * (toks // 128) * 129
                            ],
                            lambda pp: vimg,
                            count=False,
                        )
                    elif p == "k":
                        for tc in range(nt):
                            g.wait_ge(S["act"], ac_b[j] + 2 * tc + 2)
                            self.dual_dma(
                                g,
                                lambda pp, h=h, tc=tc: self.k_b_sh.ap()[h][
                                    :, pp * toks + tc * 512 : pp * toks + (tc + 1) * 512
                                ],
                                lambda pp, tc=tc: part_out[:, tc * 512 : (tc + 1) * 512],
                                count=False,
                            )
                    else:
                        for tc in range(nt):
                            g.wait_ge(S["act"], ac_b[j] + 2 * tc + 2)
                            g.dma_start(
                                out=self.q_b_loc.ap()[h][:, tc * 512 : (tc + 1) * 512],
                                in_=part_out[:, tc * 512 : (tc + 1) * 512],
                            ).then_inc(S["st"], 16)

        for nm, tbl in (("mm", mm_b), ("act", ac_b), ("dve", dv_b), ("tp", tp_b),
                        ("st", st_b)):
            ctr.cnt[nm] = tbl[len(jobs)]
        ctr.cnt["ld"] = ld_b[len(jobs)] + 256
        self.cbf.pop()
        self.cf32.pop()
        self.bar_block()

    def ph_signal(self, k):
        S, ctr = self.S, self
        b = self.snap()
        with self.nc.Block() as blk:

            @blk.gpsimd
            def _(g):
                g.wait_ge(S["st"], b["st"])
                self.signal(g, k)

        self.bar_block()

    def ph9(self):
        S, ctr, nc = self.S, self, self.nc
        self.cbf.off = self._xT_base
        self.cbf.push()
        self.cf32.push()
        K_sb = [self.cbf.take(SEQ), self.cbf.take(SEQ)]
        V_sb = [self.cbf.take(64 * 129), self.cbf.take(64 * 129)]
        q_sb = [self.cbf.take(QUAR), self.cbf.take(QUAR)]
        e_sb = [self.cbf.take(3 * 512), self.cbf.take(3 * 512)]
        sqbf = self.cbf.take(512)
        nB = self.cbf.take(1)
        aoT_sb = [self.cbf.take(128), self.cbf.take(128)]
        ao_f = [self.cf32.take(128), self.cf32.take(128)]
        nrm = self.cmisc.take(4)
        bvec = self.cmisc.take(1)
        rdv = self.cmisc.take(1)

        NH9 = int(os.environ.get("MK_PH9H", str(H)))
        KCH = [list(range(c, min(c + 3, 64))) for c in range(0, 64, 3)]
        NCH = len(KCH)

        b = self.snap()
        marks = []
        cnt = dict(mm=b["mm"], act=b["act"], dve=b["dve"], tp=b["tp"], st=b["st"],
                   ld=b["ld"])
        for h in range(NH9):
            e = {}
            cnt["ld"] += 48
            e["ld"] = cnt["ld"]
            e["mm_nrm"] = []
            e["act_nrm"] = []
            e["dve_nrm"] = []
            for ch in range(20):
                cnt["act"] += 1
                e["act_nrm"].append(cnt["act"])
                cnt["mm"] += 1
                e["mm_nrm"].append(cnt["mm"])
                cnt["dve"] += 2
                e["dve_nrm"].append(cnt["dve"])
            cnt["dve"] += 4
            e["dve_B"] = cnt["dve"] - 1
            cnt["act"] += 1
            e["act_B"] = cnt["act"]
            cnt["dve"] += 1
            e["dve_nB"] = cnt["dve"]
            cnt["mm"] += 1
            e["mm_bc"] = cnt["mm"]
            cnt["dve"] += 1
            e["dve_bc"] = cnt["dve"]
            e["qc"] = []
            for qc in range(4):
                q = {}
                q["mm_sc"] = []
                q["act_e"] = []
                q["mm_av"] = []
                for ci, ch in enumerate(KCH):
                    cnt["mm"] += 1
                    q["mm_sc"].append(cnt["mm"])
                    cnt["act"] += len(ch)
                    q["act_e"].append(cnt["act"])
                    cnt["mm"] += 1
                    q["mm_av"].append(cnt["mm"])
                q["dve_ts"] = []
                for tq in range(4):
                    cnt["dve"] += 2
                    q["dve_ts"].append(cnt["dve"])
                cnt["tp"] += 4
                q["tp"] = cnt["tp"]
                q["dve_ev"] = []
                for tq in range(4):
                    cnt["dve"] += 1
                    q["dve_ev"].append(cnt["dve"])
                cnt["st"] += 64
                q["st"] = cnt["st"]
                e["qc"].append(q)
            marks.append(e)
        tot = dict(cnt)

        with (
            nc.psum_tensor("s9a", [128, 512], F32) as s9a,
            nc.psum_tensor("s9b", [128, 512], F32) as s9b,
            nc.psum_tensor("s9c", [128, 512], F32) as s9c,
            nc.psum_tensor("s9d", [128, 512], F32) as s9d,
            nc.psum_tensor("s9e", [128, 512], F32) as s9e,
            nc.psum_tensor("s9f", [128, 512], F32) as s9f,
            nc.psum_tensor("pv9a", [128, 512], F32) as pv9a,
            nc.psum_tensor("pv9b", [128, 512], F32) as pv9b,
            nc.Block() as blk,
        ):
            psc = [[s9a, s9b, s9c], [s9d, s9e, s9f]]
            pav = [pv9a, pv9b]

            @blk.sync
            def _(e):
                for h in range(NH9):
                    if h >= 2:
                        e.wait_ge(S["mm"], marks[h - 2]["qc"][3]["mm_av"][-1])
                    e.dma_start(out=K_sb[h % 2], in_=self.k_b_sh.ap()[h]).then_inc(
                        S["ld"], 16
                    )
                    e.dma_start(out=V_sb[h % 2], in_=self.v_b_sh.ap()[h]).then_inc(
                        S["ld"], 16
                    )
                    e.dma_start(out=q_sb[h % 2], in_=self.q_b_loc.ap()[h]).then_inc(
                        S["ld"], 16
                    )

            @blk.scalar
            def _(a):
                for h in range(NH9):
                    e = marks[h]
                    a.wait_ge(S["ld"], e["ld"])
                    for ch in range(20):
                        if ch >= 1:
                            a.wait_ge(S["mm"], e["mm_nrm"][ch - 1])
                        src = (
                            K_sb[h % 2][:, ch * 512 : (ch + 1) * 512]
                            if ch < 16
                            else q_sb[h % 2][:, (ch - 16) * 512 : (ch - 15) * 512]
                        )
                        a.activation(sqbf, src, AF.Square).then_inc(S["act"], 1)
                    a.wait_ge(S["dve"], e["dve_B"])
                    a.activation(nrm[0:1, 3:4], nrm[0:1, 2:3], AF.Sqrt).then_inc(
                        S["act"], 1
                    )
                    for qc in range(4):
                        q = e["qc"][qc]
                        for ci, ch in enumerate(KCH):
                            a.wait_ge(S["mm"], q["mm_sc"][ci])
                            a.wait_ge(S["dve"], e["dve_bc"])
                            if ci >= 2:
                                a.wait_ge(S["mm"], q["mm_av"][ci - 2])
                            elif qc >= 1:
                                a.wait_ge(S["mm"], e["qc"][qc - 1]["mm_av"][NCH - 2 + ci])
                            elif h >= 1:
                                a.wait_ge(
                                    S["mm"],
                                    marks[h - 1]["qc"][3]["mm_av"][NCH - 2 + ci],
                                )
                            for kk in range(len(ch)):
                                a.activation(
                                    e_sb[ci % 2][:, kk * 512 : (kk + 1) * 512],
                                    psc[ci % 2][kk][:, :],
                                    AF.Exp,
                                    bias=bvec,
                                    scale=SCALE_B,
                                ).then_inc(S["act"], 1)

            @blk.vector
            def _(v):
                for h in range(NH9):
                    e = marks[h]
                    dv = b["dve"] if h == 0 else marks[h - 1]["qc"][3]["dve_ev"][-1]
                    for ch in range(20):
                        v.wait_ge(S["mm"], e["mm_nrm"][ch])
                        dst = nrm[0:1, 1:2] if ch < 16 else nrm[0:1, 0:1]
                        if ch == 0 or ch == 16:
                            v.reduce_max(dst, psc[0][2][0:1, :], axis=AX.X).then_inc(
                                S["dve"], 1
                            )
                            v.sem_inc(S["dve"], 1)
                            dv += 2
                        else:
                            v.reduce_max(
                                nrm[0:1, 2:3], psc[0][2][0:1, :], axis=AX.X
                            ).then_inc(S["dve"], 1)
                            dv += 1
                            v.wait_ge(S["dve"], dv)
                            v.tensor_max(dst, dst, nrm[0:1, 2:3]).then_inc(
                                S["dve"], 1
                            )
                            dv += 1
                    v.wait_ge(S["dve"], dv)
                    v.tensor_mul(
                        nrm[0:1, 2:3], nrm[0:1, 0:1], nrm[0:1, 1:2]
                    ).then_inc(S["dve"], 1)
                    v.sem_inc(S["dve"], 1)
                    v.sem_inc(S["dve"], 1)
                    dv += 3
                    v.wait_ge(S["act"], e["act_B"])
                    v.tensor_scalar_mul(
                        nrm[0:1, 2:3], nrm[0:1, 3:4], -SCALE_B
                    ).then_inc(S["dve"], 1)
                    dv += 1
                    v.wait_ge(S["dve"], dv)
                    v.tensor_copy(nB[0:1, :], nrm[0:1, 2:3]).then_inc(S["dve"], 1)
                    dv += 1
                    v.wait_ge(S["mm"], e["mm_bc"])
                    v.tensor_copy(bvec, pav[0][:, 0:1]).then_inc(S["dve"], 1)
                    dv += 1
                    for qc in range(4):
                        q = e["qc"][qc]
                        v.wait_ge(S["mm"], q["mm_av"][-1])
                        for tq in range(4):
                            col = (tq % 2) * 129
                            v.reciprocal(
                                rdv, pav[tq // 2][:, col + 128 : col + 129]
                            ).then_inc(S["dve"], 1)
                            dv += 1
                            v.wait_ge(S["dve"], dv)
                            v.tensor_scalar_mul(
                                ao_f[tq % 2], pav[tq // 2][:, col : col + 128], rdv
                            ).then_inc(S["dve"], 1)
                            dv += 1
                        for tq in range(4):
                            v.wait_ge(S["tp"], q["tp"] - 3 + tq)
                            v.wait_ge(S["st"], max(b["st"], q["st"] - 80 + 16 * tq))
                            v.tensor_copy(
                                aoT_sb[tq % 2], psc[tq % 2][0][:, 0:128]
                            ).then_inc(S["dve"], 1)
                            dv += 1

            @blk.tensor
            def _(t):
                for h in range(NH9):
                    e = marks[h]
                    t.wait_ge(S["ld"], e["ld"])
                    for ch in range(20):
                        t.wait_ge(S["act"], e["act_nrm"][ch])
                        if ch >= 1:
                            t.wait_ge(S["dve"], e["dve_nrm"][ch - 1])
                        t.matmul(
                            psc[0][2][0:1, :], self.ones_col, sqbf,
                            start=True, stop=True,
                        ).then_inc(S["mm"], 1)
                    t.wait_ge(S["dve"], e["dve_nB"])
                    t.matmul(
                        pav[0][:, 0:1], self.ones_row_p[0:1, :], nB[0:1, :],
                        start=True, stop=True,
                    ).then_inc(S["mm"], 1)
                    for qc in range(4):
                        q = e["qc"][qc]
                        for ci, ch in enumerate(KCH):
                            if ci >= 2:
                                t.wait_ge(S["act"], q["act_e"][ci - 2])
                            elif qc >= 1:
                                t.wait_ge(S["act"], e["qc"][qc - 1]["act_e"][NCH - 2 + ci])
                                t.wait_ge(S["dve"], e["qc"][qc - 1]["dve_ev"][-1])
                            elif h >= 1:
                                t.wait_ge(
                                    S["act"],
                                    marks[h - 1]["qc"][3]["act_e"][NCH - 2 + ci],
                                )
                                t.wait_ge(S["dve"], marks[h - 1]["qc"][3]["dve_ev"][-1])
                            if qc == 0 and ci == 0:
                                t.wait_ge(S["dve"], e["dve_bc"])
                            for kk, kt in enumerate(ch):
                                mi = t.matmul(
                                    psc[ci % 2][kk][:, :],
                                    K_sb[h % 2][:, kt * 128 : (kt + 1) * 128],
                                    q_sb[h % 2][:, qc * 512 : (qc + 1) * 512],
                                    start=True,
                                    stop=True,
                                )
                                if kk == len(ch) - 1:
                                    mi.then_inc(S["mm"], 1)
                            t.wait_ge(S["act"], q["act_e"][ci])
                            if ci == 0:
                                prevn = (
                                    e["qc"][qc - 1]["dve_ts"][-1]
                                    if qc >= 1
                                    else (
                                        marks[h - 1]["qc"][3]["dve_ts"][-1]
                                        if h >= 1
                                        else 0
                                    )
                                )
                                if prevn:
                                    t.wait_ge(S["dve"], prevn)
                            for kk, kt in enumerate(ch):
                                gk = 3 * ci + kk
                                for tq in range(4):
                                    col = (tq % 2) * 129
                                    mi = t.matmul(
                                        pav[tq // 2][:, col : col + 129],
                                        e_sb[ci % 2][
                                            :,
                                            kk * 512 + tq * 128 : kk * 512 + (tq + 1) * 128,
                                        ],
                                        V_sb[h % 2][:, kt * 129 : kt * 129 + 129],
                                        start=(gk == 0 and tq % 2 == 0),
                                        stop=(gk == 63 and tq % 2 == 1),
                                    )
                                    if kk == len(ch) - 1 and tq == 3:
                                        mi.then_inc(S["mm"], 1)
                        for tq in range(4):
                            t.wait_ge(S["dve"], q["dve_ts"][tq])
                            if tq >= 2:
                                t.wait_ge(S["dve"], q["dve_ev"][tq - 2])
                            t.transpose(
                                psc[tq % 2][0][:, 0:128],
                                ao_f[tq % 2],
                                self.identf_sb,
                            ).then_inc(S["tp"], 1)

            @blk.gpsimd
            def _(g):
                for h in range(NH9):
                    e = marks[h]
                    for qc in range(4):
                        q = e["qc"][qc]
                        for tq in range(4):
                            g.wait_ge(S["dve"], q["dve_ev"][tq])
                            g.dma_start(
                                out=self.aoT_b.ap()[qc * 4 + tq, h, :, :],
                                in_=aoT_sb[tq % 2],
                            ).then_inc(S["st"], 16)

        for nm in ("mm", "act", "dve", "tp", "st", "ld"):
            ctr.cnt[nm] = tot[nm]
        self.cbf.pop()
        self.cf32.pop()
        self.bar_block()

    def ph10(self):
        S, ctr, nc = self.S, self, self.nc
        self.cbf.off = self._xT_base
        self.cbf.push()
        self.cf32.push()
        woT = self.cbf.take(16 * 2048)
        ao_sb = [self.cbf.take(16 * 512), self.cbf.take(16 * 512)]
        BoT_sb = self.cbf.take(256)
        AoT_sb = self.cbf.take(2048)
        tmp_bf = self.cbf.take(512)
        bo_row = self.cbf.take(2048)
        xr = [self.cf32.take(2048), self.cf32.take(2048)]
        y = [self.cf32.take(512), self.cf32.take(512)]

        b = self.snap()
        NG = 4

        with (
            nc.psum_tensor("pa10", [128, 512], F32) as pa10,
            nc.psum_tensor("pb10", [128, 512], F32) as pb10,
            nc.psum_tensor("pt10", [128, 512], F32) as pt10,
            nc.Block() as blk,
        ):
            pacc = [pa10, pb10]

            @blk.sync
            def _(e):
                for k in range(16):
                    e.dma_start(
                        out=woT[:, k * 2048 : (k + 1) * 2048].rearrange(
                            "p (o c) -> p o c", c=128
                        ),
                        in_=self.wbt["o"].ap()[:, k, :, :].rearrange(
                            "o p c -> p o c"
                        ),
                    ).then_inc(S["ld"], 16)
                e.dma_start(
                    out=BoT_sb.rearrange("p (k r) -> p k r", r=16),
                    in_=self.loraBt["o"].ap().rearrange("k p r -> p k r"),
                ).then_inc(S["ld"], 16)
                e.dma_start(
                    out=AoT_sb[0:16, :].rearrange("r (o c) -> r o c", c=128),
                    in_=self.loraAt["o"].ap().rearrange("o r c -> r o c"),
                ).then_inc(S["ld"], 16)
                e.dma_start(
                    out=xr[0][0:1, :], in_=self.bo_bc_d.ap()[0:1, :]
                ).then_inc(S["ld"], 16)
                for g in range(NG):
                    if g >= 2:
                        e.wait_ge(S["mm"], b["mm"] + 17 * (g - 1))
                    for k in range(16):
                        e.dma_start(
                            out=ao_sb[g % 2][:, k * 512 : (k + 1) * 512].rearrange(
                                "p (t c) -> p t c", c=128
                            ),
                            in_=self.aoT_b.ap()[4 * g : 4 * g + 4, k, :, :].rearrange(
                                "t p c -> p t c"
                            ),
                        ).then_inc(S["ld"], 16)
                    for qt in range(4):
                        tg = 4 * g + qt
                        if tg == 0:
                            e.wait_ge(S["act"], b["act"] + 1)
                        if tg >= 2:
                            e.wait_ge(S["dve"], b["dve"] + 4 * (tg - 1))
                        e.dma_start(
                            out=xr[tg % 2], in_=self.x3q_d.ap()[tg]
                        ).then_inc(S["ld"], 16)

            @blk.scalar
            def _(a):
                a.wait_ge(S["ld"], b["ld"] + 304)
                a.activation(bo_row[0:1, :], xr[0][0:1, :], AF.Copy).then_inc(
                    S["act"], 1
                )
                for g in range(NG):
                    a.wait_ge(S["mm"], b["mm"] + 17 * g + 1)
                    a.activation(
                        tmp_bf[0:16, :], pt10[0:16, :], AF.Copy, scale=LORA_SC
                    ).then_inc(S["act"], 1)

            @blk.tensor
            def _(t):
                t.wait_ge(S["ld"], b["ld"] + 304)
                for g in range(NG):
                    t.wait_ge(S["ld"], b["ld"] + 304 + 320 * g + 256)
                    if g >= 1:
                        t.wait_ge(S["act"], b["act"] + 1 + g)
                    for k in range(16):
                        mi = t.matmul(
                            pt10[0:16, :],
                            BoT_sb[:, k * 16 : (k + 1) * 16],
                            ao_sb[g % 2][:, k * 512 : (k + 1) * 512],
                            start=(k == 0),
                            stop=(k == 15),
                        )
                        if k == 15:
                            mi.then_inc(S["mm"], 1)
                    t.wait_ge(S["act"], b["act"] + 2 + g)
                    for u in range(16):
                        qt, oc4 = u // 4, u % 4
                        t.wait_ge(S["dve"], max(b["dve"], b["dve"] + 16 * g + u - 1))
                        for k in range(16):
                            t.matmul(
                                pacc[u % 2][:, :],
                                ao_sb[g % 2][:, k * 512 + qt * 128 : k * 512 + (qt + 1) * 128],
                                woT[:, k * 2048 + oc4 * 512 : k * 2048 + (oc4 + 1) * 512],
                                start=(k == 0),
                                stop=False,
                            )
                        t.matmul(
                            pacc[u % 2][:, :],
                            tmp_bf[0:16, qt * 128 : (qt + 1) * 128],
                            AoT_sb[0:16, oc4 * 512 : (oc4 + 1) * 512],
                            start=False,
                            stop=False,
                        )
                        t.matmul(
                            pacc[u % 2][:, :],
                            self.ones_row_p[0:1, :],
                            bo_row[0:1, oc4 * 512 : (oc4 + 1) * 512],
                            start=False,
                            stop=True,
                        ).then_inc(S["mm"], 1)

            @blk.vector
            def _(v):
                for g in range(NG):
                    for u in range(16):
                        qt, oc4 = u // 4, u % 4
                        tg = 4 * g + qt
                        v.wait_ge(S["mm"], b["mm"] + 17 * g + 2 + u)
                        v.wait_ge(
                            S["ld"], b["ld"] + 304 + 320 * g + 256 + 16 * (qt + 1)
                        )
                        v.wait_ge(S["st"], max(b["st"], b["st"] + 16 * (16 * g + u - 1)))
                        v.tensor_add(
                            y[u % 2],
                            pacc[u % 2][:, :],
                            xr[tg % 2][:, oc4 * 512 : (oc4 + 1) * 512],
                        ).then_inc(S["dve"], 1)

            @blk.gpsimd
            def _(g_):
                for g in range(NG):
                    for u in range(16):
                        qt, oc4 = u // 4, u % 4
                        tg = 4 * g + qt
                        g_.wait_ge(S["dve"], b["dve"] + 16 * g + u + 1)
                        g_.dma_start(
                            out=self.x4q.ap()[tg][:, oc4 * 512 : (oc4 + 1) * 512],
                            in_=y[u % 2],
                        ).then_inc(S["st"], 16)

        ctr.add("ld", 304 + NG * 320)
        ctr.add("mm", NG * 17)
        ctr.add("act", 1 + NG)
        ctr.add("dve", NG * 16)
        ctr.add("st", NG * 16 * 16)
        self.cbf.pop()
        self.cf32.pop()
        self.bar_block()

    def _dump_dram(self, src, dst):
        S, ctr = self.S, self
        self.cbf.push()
        buf = self.cbf.take(4096)
        n_el = int(np.prod(src.shape))
        n_el_d = int(np.prod(dst.shape))
        assert n_el == n_el_d, (src.shape, dst.shape)
        CH = 128 * 4096
        b = self.snap()
        nch = (n_el + CH - 1) // CH
        with self.nc.Block() as blk:

            @blk.gpsimd
            def _(g):
                sap = src.ap().flatten()
                dap = dst.ap().flatten()
                for i in range(nch):
                    o = i * CH
                    n = min(CH, n_el - o)
                    p = n // 4096 if n % 4096 == 0 else None
                    if p:
                        g.dma_start(
                            out=buf[0:p, :],
                            in_=sap[o : o + n].rearrange("(p c) -> p c", c=4096),
                        ).then_inc(S["st"], 16)
                        g.wait_ge(S["st"], b["st"] + 32 * i + 16)
                        g.dma_start(
                            out=dap[o : o + n].rearrange("(p c) -> p c", c=4096),
                            in_=buf[0:p, :],
                        ).then_inc(S["st"], 16)
                        g.wait_ge(S["st"], b["st"] + 32 * i + 32)
                    else:
                        g.dma_start(out=buf[0:1, 0 : n], in_=sap[o : o + n].rearrange("(o c) -> o c", o=1)).then_inc(S["st"], 16)
                        g.wait_ge(S["st"], b["st"] + 32 * i + 16)
                        g.dma_start(out=dap[o : o + n].rearrange("(o c) -> o c", o=1), in_=buf[0:1, 0 : n]).then_inc(S["st"], 16)
                        g.wait_ge(S["st"], b["st"] + 32 * i + 32)

        ctr.add("st", 32 * nch)
        self.cbf.pop()
        self.bar_block()

    def _dump_sbuf(self, view, dst):
        S, ctr = self.S, self
        b = self.snap()
        with self.nc.Block() as blk:

            @blk.gpsimd
            def _(g):
                g.dma_start(out=dst.ap(), in_=view).then_inc(S["st"], 16)
                g.wait_ge(S["st"], b["st"] + 16)

        ctr.add("st", 16)
        self.bar_block()

    def finish(self):
        S, ctr = self.S, self
        with self.nc.Block() as blk:

            @blk.gpsimd
            def _(g):
                g.wait_ge(S["st"], ctr.get("st"))
                g.wait_ge(S["ld"], ctr.get("ld"))

        self.bar_block()
        self.ctx.close()
        return self.nc, self.dbg_outs


def build(debug=(), stage=99):
    bl = Builder(debug)
    bl.ph0a()
    if stage >= 1:
        bl.ph0b()
    if stage >= 2:
        bl.ph0c(sub=1 if stage == 2 else 3)
    if stage >= 3:
        bl.ph1()
    if stage >= 4:
        bl.ph2()
    if stage >= 5:
        bl.ph3(1)
        bl.ph4()
    if stage >= 6:
        bl.ph5()
    if stage >= 7:
        bl.ph6a(bl.x2T_d, bl.hs_d, HALF)
        bl.ph6b(bl.hs_d, bl.xres2, bl.xres3, HALF)
        bl.ph7(bl.xres3, bl.x3T_d, HT)
        bl.ph7q()
    if stage >= 8:
        bl.ph8(bl.x3T_d, HALF, ("k", "v"))
        bl.ph_signal(2)
        bl.ph8(bl.x3Tq_d, QUAR, ("q",))
        bl.ph3(2)
    if stage >= 9:
        bl.ph9()
    if stage >= 10:
        bl.ph10()
        bl.ph7(bl.x4q, bl.x4T_d, QT)
    if stage >= 11:
        bl.ph6a(bl.x4T_d, bl.hs2_d, QUAR)
        bl.ph6b(bl.hs2_d, bl.x4q, bl.y_out, QUAR)
    return bl.finish()



def _core_inputs(inputs, c, epoch):
    parity = c % 2
    batch = c // 4
    pairidx = (c % 4) // 2
    blocks = EVEN_BLOCKS if parity == 0 else ODD_BLOCKS
    x = inputs["L0_attn_a_x"]
    x_own = np.concatenate(
        [np.asarray(x[batch, 512 * bb : 512 * bb + 512]) for bb in blocks], axis=0
    )
    rope, masks = _host_tables(parity)
    m = {
        "x_own": np.ascontiguousarray(x_own, dtype=np.float32),
        "w_qkv": inputs["L0_attn_a_w_qkv"],
        "w_out": inputs["L0_attn_a_w_out"],
        "ln_scale": inputs["L0_attn_a_ln_scale"],
        "ln_bias": inputs["L0_attn_a_ln_bias"],
        "w_gate": inputs["L0_ffn_w_gate"],
        "w_up": inputs["L0_ffn_w_up"],
        "w_down": inputs["L0_ffn_w_down"],
        "rope": rope,
        "masks": masks,
        "cfg": np.array([[parity, pairidx, epoch, 0]], dtype=np.int32),
    }
    for n in "qkvo":
        m[f"W{n}"] = inputs[f"L0_attn_b_W{n}"]
        m[f"b{n}"] = inputs[f"L0_attn_b_b{n}"]
        m[f"B{n}"] = inputs[f"L0_attn_b_B{n}"]
        m[f"A{n}"] = inputs[f"L0_attn_b_A{n}"]
    return m


def _assemble(results):
    out = np.empty((2, SEQ, D), dtype=np.float32)
    for c in range(8):
        parity = c % 2
        batch = c // 4
        pairidx = (c % 4) // 2
        blocks = (EVEN_BLOCKS if parity == 0 else ODD_BLOCKS)[
            4 * pairidx : 4 * pairidx + 4
        ]
        yq = np.asarray(results[c]["y"]).reshape(QUAR, D)
        for bi, bb in enumerate(blocks):
            out[batch, 512 * bb : 512 * bb + 512] = yq[512 * bi : 512 * bi + 512]
    return out


def kernel(**inputs):
    epoch = random.randint(1, 1 << 24)
    nc, _ = build()
    in_maps = [_core_inputs(inputs, c, epoch) for c in range(8)]
    res = run_bass_kernel_spmd(nc, in_maps, core_ids=list(range(8)))
    return _assemble(res.results)
